# version 34
# baseline (speedup 1.0000x reference)
import numpy as np

import concourse.bass as bass
import concourse.mybir as mybir
from concourse.bass_utils import run_bass_kernel_spmd
from concourse.bass_isa import ReduceOp

F32 = mybir.dt.float32
BF16 = mybir.dt.bfloat16
ALU = mybir.AluOpType
AF = mybir.ActivationFunctionType
AXX = mybir.AxisListType.X

B, T, C, I = 4, 4096, 1024, 2816
NCORES = 8
CC = C // 128
QMAX = 127.0
LN_EPS = 1e-5

ENGINES = ("sync", "scalar", "vector", "tensor", "gpsimd")


class Prog:

    def __init__(self):
        self.steps = []
        self.eng_cnt = {e: 0 for e in ENGINES}
        self.last_w = {}
        self.readers = {}
        self.dma_tot = {}
        self.sem_names = set("E_" + e for e in ENGINES)
        self.pending = {e: {} for e in ENGINES}

    def _dep_waits(self, reads, writes):
        waits = {}

        def add(s, v):
            if waits.get(s, 0) < v:
                waits[s] = v

        for k in reads:
            d = self.last_w.get(k)
            if d:
                add(*d)
        for k in writes:
            d = self.last_w.get(k)
            if d:
                add(*d)
            for s, v in self.readers.get(k, {}).items():
                add(s, v)
        return waits

    def step(self, engine, emit, reads=(), writes=()):
        waits = self._dep_waits(reads, writes)
        if engine == "gpsimd" and self.eng_cnt[engine] > 0:
            if waits.get("E_gpsimd", 0) < self.eng_cnt[engine]:
                waits["E_gpsimd"] = self.eng_cnt[engine]
        if self.pending[engine]:
            for s, v in self.pending[engine].items():
                if waits.get(s, 0) < v:
                    waits[s] = v
            self.pending[engine] = {}
        self.eng_cnt[engine] += 1
        val = self.eng_cnt[engine]
        sem = "E_" + engine
        self.steps.append((engine, waits, emit, sem, False))
        for k in writes:
            self.last_w[k] = (sem, val)
            self.readers[k] = {}
        for k in reads:
            self.readers.setdefault(k, {})[sem] = val

    def dma(self, engine, ch, pairs, reads=(), writes=(), slow=False):
        sem = "D_" + ch
        self.sem_names.add(sem)
        waits = self._dep_waits(reads, writes)
        if self.pending[engine]:
            for s_, v_ in self.pending[engine].items():
                if waits.get(s_, 0) < v_:
                    waits[s_] = v_
            self.pending[engine] = {}
        prev = self.dma_tot.get(sem, 0)
        if prev and waits.get(sem, 0) < prev:
            waits[sem] = prev
        tot = prev + 16 * len(pairs)
        self.dma_tot[sem] = tot

        def emit(eng, s, pairs=pairs, slow=slow):
            for o, i in pairs:
                eng.dma_start(out=o, in_=i,
                              allow_slow_non_contiguous=slow).then_inc(s, 16)

        self.steps.append((engine, waits, emit, sem, True))
        for k in writes:
            self.last_w[k] = (sem, tot)
            self.readers[k] = {}
        for k in reads:
            self.readers.setdefault(k, {})[sem] = tot

    def barrier(self):
        snap = {("E_" + e): self.eng_cnt[e] for e in ENGINES if self.eng_cnt[e] > 0}
        snap.update({s: v for s, v in self.dma_tot.items() if v > 0})
        for e in ENGINES:
            w = dict(self.pending[e])
            for s, v in snap.items():
                if w.get(s, 0) < v:
                    w[s] = v
            self.pending[e] = w
        self.last_w = {}
        self.readers = {}

    def emit_all(self, nc, block):
        import contextlib
        stack = contextlib.ExitStack()
        sems = {n: stack.enter_context(nc.semaphore(n)) for n in sorted(self.sem_names)}
        watermark = {e: {} for e in ENGINES}

        def run_engine(eng, ename):
            wm = watermark[ename]
            for engine, waits, emit, sem, is_dma in self.steps:
                if engine != ename:
                    continue
                for s in sorted(waits):
                    v = waits[s]
                    if wm.get(s, 0) < v:
                        eng.wait_ge(sems[s], v)
                        wm[s] = v
                if emit is None:
                    eng.sem_inc(sems[sem], 1)
                elif is_dma:
                    emit(eng, sems[sem])
                else:
                    insts = emit(eng)
                    last = insts[-1] if isinstance(insts, (list, tuple)) else insts
                    last.then_inc(sems[sem], 1)

        for ename in ENGINES:
            getattr(block, ename)(lambda eng, ename=ename: run_engine(eng, ename))
        return stack


def build_nc(tcore=T * B // NCORES, icdim=I, dbg=False):
    ic_n = icdim // 128
    tslab = min(512, tcore)
    nslab = tcore // tslab
    ngrp = tslab // 128
    nhalf = 2 if tslab >= 512 else 1
    thalf = tslab // nhalf
    ngq = tcore // 128

    nc = bass.Bass("TRN2", target_bir_lowering=False)

    x_in = nc.dram_tensor("x", [tcore, C], F32, kind="ExternalInput").ap()
    xb_in = nc.dram_tensor("xbound", [1, C], F32, kind="ExternalInput").ap()
    out_hbm = nc.dram_tensor("out", [tcore, C], F32, kind="ExternalOutput").ap()
    w_in = {nm: nc.dram_tensor(nm, [C, C], F32, kind="ExternalInput").ap()
            for nm in ("Wk", "Wv", "Wr", "Wo")}
    wd_in = nc.dram_tensor("wd", [C, icdim], F32, kind="ExternalInput").ap()
    wgu_in = nc.dram_tensor("wgu_mine", [(2 * ic_n + 7) // 8 * 128, C], F32,
                            kind="ExternalInput").ap()
    wdm_in = nc.dram_tensor("wd_mine", [C, (ic_n + 7) // 8 * 128], F32,
                            kind="ExternalInput").ap()
    wdsc_in = nc.dram_tensor("wdsc_mine", [128, icdim], F32,
                             kind="ExternalInput").ap()
    vec_in = {n: nc.dram_tensor(n, [C], F32, kind="ExternalInput").ap()
              for n in ("tmk", "tmv", "tmr", "tdecay", "tfirst", "lnw", "lnb")}
    mprev_in = nc.dram_tensor("mprev", [128, 1], F32, kind="ExternalInput").ap()
    selfhot_in = nc.dram_tensor("selfhot", [128, 8], F32, kind="ExternalInput").ap()
    csel_in = nc.dram_tensor("csel", [128, 8], F32, kind="ExternalInput").ap()
    clipx_in = nc.dram_tensor("clipx", [128, 1], F32, kind="ExternalInput").ap()
    clipd_in = nc.dram_tensor("clipd", [128, 1], F32, kind="ExternalInput").ap()

    SCRK = "ExternalOutput" if dbg else "Internal"
    ngu = 2 * ic_n
    nslot_gu = (ngu + 7) // 8
    nslot_d = (ic_n + 7) // 8
    sc_t1 = nc.dram_tensor("sc_t1", [CC, 128, tcore], BF16, kind=SCRK).ap()
    sc_rv = nc.dram_tensor("sc_rv", [CC, 128, tcore], BF16, kind=SCRK).ap()
    sc_r2 = nc.dram_tensor("sc_r2", [CC, 128, tcore], BF16, kind="Internal").ap()
    sc_gu_mine = nc.dram_tensor("sc_gu_mine", [nslot_gu, 128, C], BF16,
                                kind="Internal").ap()
    sc_gu_all = nc.dram_tensor("sc_gu_all", [8, nslot_gu, 128, C], BF16,
                               kind="Internal", addr_space="Shared").ap()
    sc_scm_i = nc.dram_tensor("sc_scm_i", [128, 8 * nslot_gu], F32,
                              kind="Internal").ap()
    sc_scm_o = nc.dram_tensor("sc_scm_o", [128, 8 * nslot_gu], F32, kind="Internal",
                              addr_space="Shared").ap()
    sc_d_mine = nc.dram_tensor("sc_d_mine", [nslot_d, 128, C], BF16,
                               kind="Internal").ap()
    sc_sd_mine = nc.dram_tensor("sc_sd_mine", [1, 128], F32, kind="Internal").ap()
    sc_sd_all = nc.dram_tensor("sc_sd_all", [8, 128], F32, kind="Internal",
                               addr_space="Shared").ap()
    sc_d_all = nc.dram_tensor("sc_d_all", [8, nslot_d, 128, C], BF16,
                              kind="Internal", addr_space="Shared").ap()
    sc_h2 = nc.dram_tensor("sc_h2", [ic_n, 128, tcore], BF16, kind=SCRK).ap()
    cc_c_i = nc.dram_tensor("cc_c_i", [128, 8 * CC], F32, kind="Internal").ap()
    cc_c_o = nc.dram_tensor("cc_c_o", [128, 8 * CC], F32, kind="Internal",
                            addr_space="Shared").ap()
    cc_m_i = [nc.dram_tensor(f"cc_m_i{j}", [1, 1], F32, kind="Internal").ap()
              for j in range(2)]
    hrow = [nc.dram_tensor(f"hrow{j}", [1, 128], F32, kind="Internal").ap()
            for j in range(2)]
    cc_m_o = [nc.dram_tensor(f"cc_m_o{j}", [1, 1], F32, kind="Internal",
                             addr_space="Shared").ap() for j in range(2)]
    dbg_h1 = (nc.dram_tensor("dbg_h1", [tcore, C], F32, kind="ExternalOutput").ap()
              if dbg else None)
    dbg_m = (nc.dram_tensor("dbg_m", [1, 4], F32, kind="ExternalOutput").ap()
             if dbg else None)
    RG = [list(range(NCORES))]

    p = Prog()
    open_t = {}

    def sb(name, shape, dt):
        tt = nc.sbuf_tensor("s_" + name, shape, dt)
        h = tt.__enter__()
        open_t[name] = tt
        return h

    def sb_free(*names):
        for name in reversed(names):
            open_t.pop(name).__exit__(None, None, None)

    def ps(name, shape, dt):
        tt = nc.psum_tensor(name, shape, dt)
        h = tt.__enter__()
        open_t[name] = tt
        return h

    cols = {n: sb("c_" + n, [128, CC], F32)
            for n in ("mk_a", "mk_b", "mv_a", "mv_b", "mr_a", "mr_b", "lnw", "lnb",
                      "decay", "first", "lnD", "carry", "cf", "bnd", "xtb")}
    raw = {n: sb("r_" + n, [128, CC], F32)
           for n in ("tmk", "tmv", "tmr", "tdecay", "tfirst")}
    mprev = sb("mprev", [128, 1], F32)
    selfhot = sb("selfhot", [128, 8], F32)
    csel = sb("csel", [128, 8], F32)
    clipx = sb("clipx", [128, 1], F32)
    clipd = sb("clipd", [128, 1], F32)
    ident = sb("ident", [128, 128], BF16)
    iota_f = sb("iota_f", [128, 128], F32)
    iota_sl = sb("iota_sl", [128, tslab], F32)
    scal = sb("scal", [128, 16], F32)
    amax1 = sb("amax1", [128, 1], F32)
    amax2 = sb("amax2", [128, 1], F32)
    swd = sb("swd", [128, CC], F32)
    scall = sb("scall", [128, 8, nslot_gu], F32)
    stats = sb("stats", [128, 2, ngrp, 2, 6], F32)
    mvar = sb("mvar", [128, 2, ngrp, 2], F32)
    rsqc = sb("rsqc", [128, 2, ngrp], F32)
    betac = sb("betac", [128, 2, ngrp], F32)
    fmask = sb("fmask", [128, 8, CC], F32)
    fall = sb("fall", [128, 8, CC], F32)
    xbnd_tm = sb("xbnd_tm", [1, C], F32)
    xbnd_b = sb("xbnd_b", [1, C], BF16)
    bstat = sb("bstat", [1, 2, 6], F32)
    epsc = sb("epsc", [128, 1], F32)
    ones_row = sb("ones_row", [1, 128], BF16)
    abf = sb("abf", [128, 1], BF16)
    amsb = sb("amsb", [1, 1], BF16)
    amsc = sb("amsc", [1, 1], F32)
    bvar = sb("bvar", [1, 2], F32)
    ident1 = sb("ident1", [1, 1], BF16)

    pp = [ps(f"pp{i}", [128, 512], F32) for i in range(4)]
    pt = [ps(f"pt{i}", [128, 1024], BF16) for i in range(2)]
    ptw = ps("ptw", [128, 1024], BF16)
    PPK = [f"pp{i}" for i in range(4)]
    PTK = [f"pt{i}" for i in range(2)]

    p.dma("sync", "vin",
          [(raw[n][:], vec_in[n].rearrange("(j q) -> q j", q=128))
           for n in ("tmk", "tmv", "tmr", "tdecay", "tfirst")] +
          [(cols["lnw"][:], vec_in["lnw"].rearrange("(j q) -> q j", q=128)),
           (cols["lnb"][:], vec_in["lnb"].rearrange("(j q) -> q j", q=128)),
           (mprev[:], mprev_in[:]), (selfhot[:], selfhot_in[:]),
           (csel[:], csel_in[:]), (clipx[:], clipx_in[:]), (clipd[:], clipd_in[:]),
           (xbnd_tm[:], xb_in[:])],
          writes=("tmk", "tmv", "tmr", "tdecay", "tfirst", "lnw", "lnb", "mprev",
                  "selfhot", "csel", "clipx", "clipd", "xbnd"), slow=True)
    p.step("scalar", lambda e: e.activation(cols["decay"][:], raw["tdecay"][:], AF.Sigmoid),
           reads=("tdecay",), writes=("decay",))
    p.step("scalar", lambda e: e.activation(cols["first"][:], raw["tfirst"][:], AF.Exp),
           reads=("tfirst",), writes=("first",))
    p.step("scalar", lambda e: e.activation(cols["lnD"][:], cols["decay"][:], AF.Ln),
           reads=("decay",), writes=("lnD",))
    for nm, rw in (("mk", "tmk"), ("mv", "tmv"), ("mr", "tmr")):
        p.step("vector", lambda e, nm=nm, rw=rw: e.tensor_tensor(
            cols[nm + "_a"][:], raw[rw][:], cols["lnw"][:], ALU.mult),
            reads=(rw, "lnw"), writes=(nm + "_a",))
        p.step("vector", lambda e, nm=nm: e.tensor_sub(
            cols[nm + "_b"][:], cols["lnw"][:], cols[nm + "_a"][:]),
            reads=(nm + "_a", "lnw"), writes=(nm + "_b",))
    p.step("vector", lambda e: e.reciprocal(cols["bnd"][:], cols["lnw"][:]),
           reads=("lnw",), writes=("bnd",))
    p.step("vector", lambda e: e.tensor_tensor(cols["bnd"][:], cols["bnd"][:],
                                               cols["lnb"][:], ALU.mult),
           reads=("bnd", "lnb"), writes=("bnd",))
    p.step("vector", lambda e: e.tensor_scalar(cols["bnd"][:], cols["bnd"][:], -1.0,
                                               None, ALU.mult),
           reads=("bnd",), writes=("bnd",))
    p.step("gpsimd", lambda e: e.iota(iota_f[:], [[1, 128]], channel_multiplier=-1,
                                      allow_small_or_imprecise_dtypes=True),
           writes=("iota_f",))
    p.step("gpsimd", lambda e: e.tensor_scalar(ident[:], iota_f[:], 0.0, None,
                                               ALU.is_equal),
           reads=("iota_f",), writes=("ident",))
    p.step("gpsimd", lambda e: e.memset(ident1[:], 1.0), writes=("ident1",))
    p.step("gpsimd", lambda e: e.memset(ones_row[:], 1.0), writes=("ones_row",))
    p.step("gpsimd", lambda e: e.iota(iota_sl[:], [[1, tslab]], base=1,
                                      channel_multiplier=0,
                                      allow_small_or_imprecise_dtypes=True),
           writes=("iota_sl",))
    p.step("gpsimd", lambda e: e.memset(epsc[:], LN_EPS), writes=("epsc",))

    p.step("vector", lambda e: [e.bn_stats(bstat[:, 0, :], xbnd_tm[:, :512]),
                                e.bn_stats(bstat[:, 1, :], xbnd_tm[:, 512:])][-1],
           reads=("xbnd",), writes=("bstat",))
    p.step("vector", lambda e: e.bn_aggr(bvar[:], bstat[:]),
           reads=("bstat",), writes=("bvar",))
    p.step("scalar", lambda e: e.activation(bvar[:, 1:2], bvar[:, 1:2], AF.Sqrt,
                                            bias=epsc[0:1, :]),
           reads=("bvar", "epsc"), writes=("bvar",))
    p.step("vector", lambda e: e.reciprocal(bvar[:, 1:2], bvar[:, 1:2]),
           reads=("bvar",), writes=("bvar",))
    p.step("vector", lambda e: e.tensor_scalar(bvar[:, 0:1], bvar[:, 0:1], bvar[:, 1:2],
                                               -1.0, ALU.mult, ALU.mult),
           reads=("bvar",), writes=("bvar",))
    p.step("scalar", lambda e: e.activation(xbnd_b[:], xbnd_tm[:], AF.Identity,
                                            bias=bvar[:, 0:1], scale=bvar[:, 1:2]),
           reads=("xbnd", "bvar"), writes=("xbndb",))
    p.step("tensor", lambda e: [e.transpose(pt[0][:, 2 * cc2:2 * cc2 + 1],
                                            xbnd_b[:, cc2 * 128:(cc2 + 1) * 128],
                                            ident1[:]) for cc2 in range(CC)][-1],
           reads=("xbndb", "ident1"), writes=("pt0",))
    p.step("vector", lambda e: e.tensor_sub(
        cols["xtb"][:], pt[0][:, :2 * CC].rearrange("q (c w) -> q c w", w=2)[:, :, 0],
        cols["bnd"][:]),
        reads=("pt0", "bnd"), writes=("xtb",))
    p.step("vector", lambda e: e.scalar_tensor_tensor(
        cols["xtb"][:], cols["xtb"][:], mprev[:], cols["bnd"][:], ALU.mult, ALU.add),
        reads=("xtb", "mprev", "bnd"), writes=("xtb",))
    p.step("vector", lambda e: e.memset(cols["carry"][:], 0.0), writes=("carry",))

    wto = sb("wt_o", [128, CC, C], BF16)
    x_sb = sb("x_sb", [128, ngrp, C], F32)
    xstd = sb("xstd", [128, 2, C], BF16)
    xt_sb = sb("xt_sb", [128, CC, tslab + 1], F32)
    mixt = sb("mixt", [128, CC, thalf], BF16)
    xk_sb = sb("xk_sb", [128, 2, CC, thalf], BF16)
    xv_sb = sb("xv_sb", [128, 2, CC, thalf], BF16)
    xq_sb = sb("xq_sb", [128, 2, CC, thalf], BF16)
    k_sb = sb("k_sb", [128, 2, CC, tslab], BF16)
    r_sb = sb("r_sb", [128, CC, tslab], BF16)
    rv_sb = sb("rv_sb", [128, CC, tslab], BF16)
    s_sb = sb("s_sb", [128, 2, tslab], F32)
    t1_sb = sb("t1_sb", [128, CC, tslab], BF16)
    wt = {n: sb("wt_" + n, [128, CC, C], BF16) for n in ("k", "v", "r")}
    wld = sb("wld", [128, 2, 1024], F32)
    wcast = sb("wcast", [128, 2, 1024], BF16)
    wstage = sb("wstage", [128, 2, 1024], BF16)
    wq_v = sb("wq_v", [128, 1024], F32)
    wq_i = sb("wq_i", [128, 1024], mybir.dt.int32)
    wamax = sb("wamax", [128, 1], F32)
    wamt = sb("wamt", [128, 1], F32)
    sdcol = sb("sdcol", [128, 1], F32)
    P1_NAMES = ("x_sb", "xstd", "xt_sb", "mixt", "xk_sb", "xv_sb", "xq_sb", "k_sb",
                "r_sb", "rv_sb", "s_sb", "t1_sb", "wt_k", "wt_v", "wt_r", "wld",
                "wcast", "wstage", "wq_v", "wq_i", "wamax", "wamt", "sdcol",
                "scol_gu", "swd_inv")

    def WLT(par):
        return tuple(f"wld{par}{q}" for q in "abcd")

    def wld_load(eng, ch, par, src, w):
        h = w // 4
        for qi, q in enumerate("abcd"):
            p.dma(eng, f"{ch}{par}{q}", [(wld[:, par, qi * h:(qi + 1) * h],
                                          src[:, qi * h:(qi + 1) * h])],
                  reads=(f"wld{par}{q}",), writes=(f"wld{par}{q}",))

    def prep_attn_chunk(nm, win, wdst, wkey, dc, par):
        wld_load("sync", "wl", par, win[dc * 128:(dc + 1) * 128, :], C)
        p.step("vector", lambda e, par=par: e.tensor_copy(wcast[:, par, :],
                                                          wld[:, par, :]),
               reads=WLT(par), writes=(f"wcast{par}",))
        pk = PTK[par]
        p.step("tensor", lambda e, par=par: [
            e.transpose(pt[par][:, c2 * 128:(c2 + 1) * 128],
                        wcast[:, par, c2 * 128:(c2 + 1) * 128], ident[:])
            for c2 in range(CC)][-1],
            reads=(f"wcast{par}", "ident"), writes=(pk,))
        p.step("scalar", lambda e, dc=dc, par=par, wdst=wdst: e.activation(
            wdst[:, :, dc * 128:(dc + 1) * 128],
            pt[par][:].rearrange("q (c d) -> q c d", c=CC), AF.Identity),
            reads=(pk,), writes=(f"{wkey}{dc}",))

    scol_gu = sb("scol_gu", [128, nslot_gu], F32)
    swd_inv = sb("swd_inv", [128, CC], F32)

    wq_units = []

    def emit_down_scales():
        def unit():
            for sl3 in range(3):
                off = sl3 * 1024
                w = min(1024, icdim - off)
                if w <= 0:
                    continue
                wld_load("sync", "qdsc", sl3 % 2, wdsc_in[:, off:off + w], w)
                if sl3 == 0:
                    p.step("vector", lambda e, w=w, sl3=sl3: e.tensor_reduce(
                        wamax[:], wld[:, sl3 % 2, :w], AXX, ALU.max,
                        apply_absolute_value=True),
                        reads=WLT(sl3 % 2) + ("wamax", "wamt"), writes=("wamax",))
                else:
                    p.step("vector", lambda e, w=w, sl3=sl3: e.tensor_reduce(
                        wamt[:], wld[:, sl3 % 2, :w], AXX, ALU.max,
                        apply_absolute_value=True),
                        reads=WLT(sl3 % 2) + ("wamt",), writes=("wamt",))
                    p.step("vector", lambda e: e.tensor_max(wamax[:], wamax[:], wamt[:]),
                           reads=("wamt", "wamax"), writes=("wamax",))
            p.step("vector", lambda e: e.tensor_scalar(
                sdcol[:], wamax[:], 1.0 / QMAX, None, ALU.mult),
                reads=("wamax",), writes=("sdcol",))
            p.dma("gpsimd", "qsc", [(sc_sd_mine[0, :], sdcol[:])],
                  reads=("sdcol",), writes=("sdm",), slow=True)
            p.step("gpsimd", lambda e: e.collective_compute(
                "AllGather", ALU.bypass, RG, [sc_sd_mine[:]], [sc_sd_all[:]]),
                reads=("sdm",), writes=("ag_sd",))
            p.dma("scalar", "qdsr", [(swd[:, rc:rc + 1], sc_sd_all[rc, :])
                                     for rc in range(CC)],
                  reads=("ag_sd", "wsc_d"), writes=("wsc_d",), slow=True)
            p.step("vector", lambda e: e.reciprocal(swd_inv[:], swd[:]),
                   reads=("wsc_d", "wsd_inv"), writes=("wsd_inv",))
        unit()

    def emit_wo_units():
        def load(dc, par):
            wld_load("sync", "wl", par, w_in["Wo"][dc * 128:(dc + 1) * 128, :], C)

        def unit(dc, par):
            p.step("vector", lambda e, par=par: e.tensor_copy(wcast[:, par, :],
                                                              wld[:, par, :]),
                   reads=WLT(par), writes=(f"wcast{par}",))
            pk = PTK[par]
            p.step("tensor", lambda e, par=par: [
                e.transpose(pt[par][:, c2 * 128:(c2 + 1) * 128],
                            wcast[:, par, c2 * 128:(c2 + 1) * 128], ident[:])
                for c2 in range(CC)][-1],
                reads=(f"wcast{par}", "ident"), writes=(pk,))
            p.step("scalar", lambda e, dc=dc, par=par: e.activation(
                wto[:, :, dc * 128:(dc + 1) * 128],
                pt[par][:].rearrange("q (c d) -> q c d", c=CC), AF.Identity),
                reads=(pk,), writes=("wto",))
        for dc in range(CC):
            pu = len(wq_units) % 2
            wq_units.append((lambda dc=dc, pu=pu: load(dc, pu),
                             lambda dc=dc, pu=pu: unit(dc, pu)))

    def emit_gu_mine():
        def load(j, par):
            wld_load("sync", "ql", par, wgu_in[j * 128:(j + 1) * 128, :], C)

        def unit(j, par):
            p.step("vector", lambda e, par=par: e.tensor_reduce(
                wamax[:], wld[:, par, :C], AXX, ALU.max, apply_absolute_value=True),
                reads=WLT(par) + ("wamax", "wamt"), writes=("wamax",))
            p.step("vector", lambda e, j=j: e.tensor_scalar(
                scol_gu[:, j:j + 1], wamax[:], 1.0 / QMAX, None, ALU.mult),
                reads=("wamax",), writes=(f"scg{j}",))
            p.step("vector", lambda e: e.reciprocal(wamt[:], wamax[:]),
                   reads=("wamax",), writes=("wamt",))
            p.step("vector", lambda e, par=par: e.tensor_scalar(
                wq_v[:, :C], wld[:, par, :C], wamt[:], QMAX, ALU.mult, ALU.mult),
                reads=WLT(par) + ("wamt",), writes=("wq_v",))
            p.step("vector", lambda e: e.tensor_copy(wq_i[:, :C], wq_v[:, :C]),
                   reads=("wq_v", "wq_i"), writes=("wq_i",))
            p.step("vector", lambda e, par=par: e.tensor_copy(wcast[:, par, :C],
                                                              wq_i[:, :C]),
                   reads=("wq_i", f"wcast{par}"), writes=(f"wcast{par}",))
            p.step("tensor", lambda e, par=par: [
                e.transpose(ptw[:, c2 * 128:(c2 + 1) * 128],
                            wcast[:, par, c2 * 128:(c2 + 1) * 128], ident[:])
                for c2 in range(CC)][-1],
                reads=(f"wcast{par}", "ident", "ptw"), writes=("ptw",))
            p.step("scalar", lambda e, par=par: e.activation(
                wstage[:, par, :C], ptw[:], AF.Identity),
                reads=("ptw", f"wstage{par}"), writes=(f"wstage{par}",))
            p.dma("scalar", f"qs{par}", [(sc_gu_mine[j, :, :], wstage[:, par, :C])],
                  reads=(f"wstage{par}",), writes=(f"scm{j}",))
        for j in range(nslot_gu):
            pu = len(wq_units) % 2
            wq_units.append((lambda j=j, pu=pu: load(j, pu),
                             lambda j=j, pu=pu: unit(j, pu)))

    def emit_d_mine():
        def load(rc, par):
            wld_load("sync", "qd", par,
                     wdm_in[rc * 128:(rc + 1) * 128, :], nslot_d * 128)

        def unit(rc, par):
            for j in range(nslot_d):
                p.step("vector", lambda e, par=par, rc=rc, j=j: e.tensor_scalar(
                    wq_v[:, j * 128:(j + 1) * 128],
                    wld[:, par, j * 128:(j + 1) * 128],
                    swd_inv[:, rc:rc + 1], None, ALU.mult),
                    reads=WLT(par) + ("wsd_inv", "wq_v"), writes=("wq_v",))
            p.step("vector", lambda e: e.tensor_copy(wq_i[:, :nslot_d * 128],
                                                     wq_v[:, :nslot_d * 128]),
                   reads=("wq_v", "wq_i"), writes=("wq_i",))
            p.step("vector", lambda e, par=par: e.tensor_copy(
                wcast[:, par, :nslot_d * 128], wq_i[:, :nslot_d * 128]),
                reads=("wq_i", f"wcast{par}"), writes=(f"wcast{par}",))
            p.step("tensor", lambda e, par=par: [
                e.transpose(ptw[:, j2 * 128:(j2 + 1) * 128],
                            wcast[:, par, j2 * 128:(j2 + 1) * 128], ident[:])
                for j2 in range(nslot_d)][-1],
                reads=(f"wcast{par}", "ident", "ptw"), writes=("ptw",))
            p.step("scalar", lambda e, par=par: e.activation(
                wstage[:, par, :nslot_d * 128], ptw[:, :nslot_d * 128], AF.Identity),
                reads=("ptw", f"wstage{par}"), writes=(f"wstage{par}",))
            p.dma("scalar", f"qs{par}",
                  [(sc_d_mine[j, :, rc * 128:(rc + 1) * 128],
                    wstage[:, par, j * 128:(j + 1) * 128]) for j in range(nslot_d)],
                  reads=(f"wstage{par}",), writes=(f"scdm{rc}",))
        for rc in range(CC):
            pu = len(wq_units) % 2
            wq_units.append((lambda rc=rc, pu=pu: load(rc, pu),
                             lambda rc=rc, pu=pu: unit(rc, pu)))

    def emit_wq_allgathers():
        p.step("gpsimd", lambda e: e.collective_compute(
            "AllGather", ALU.bypass, RG, [sc_gu_mine[:]], [sc_gu_all[:]]),
            reads=tuple(f"scm{j}" for j in range(nslot_gu)), writes=("ag_gu",))
        p.step("gpsimd", lambda e: e.collective_compute(
            "AllGather", ALU.bypass, RG, [sc_d_mine[:]], [sc_d_all[:]]),
            reads=tuple(f"scdm{rc}" for rc in range(CC)), writes=("ag_d",))

    emit_wo_units()
    emit_gu_mine()
    emit_d_mine()

    wq_ci = [0]
    wq_li = [0]

    def drain_wq(n):
        for _ in range(min(n, len(wq_units) - wq_ci[0])):
            while wq_li[0] < len(wq_units) and wq_li[0] < wq_ci[0] + 2:
                wq_units[wq_li[0]][0]()
                wq_li[0] += 1
            wq_units[wq_ci[0]][1]()
            wq_ci[0] += 1
            if wq_li[0] < len(wq_units):
                wq_units[wq_li[0]][0]()
                wq_li[0] += 1



    def p1_xload(s):
        p.dma("sync", "xin", [(x_sb[:], x_in[s * tslab:(s + 1) * tslab, :]
                               .rearrange("(g q) c -> q g c", q=128))],
              reads=("xs",), writes=("xs",))

    def p1_head_ln(s):
        par = s % 2
        for g in range(ngrp):
            p.step("vector", lambda e, g=g, par=par: [
                e.bn_stats(stats[:, par, g, 0, :], x_sb[:, g, 0:512]),
                e.bn_stats(stats[:, par, g, 1, :], x_sb[:, g, 512:1024])][-1],
                reads=("xs", f"st{par}_{g}"), writes=(f"st{par}_{g}",))
            p.step("vector", lambda e, g=g, par=par: e.bn_aggr(
                mvar[:, par, g, :], stats[:, par, g, :, :]),
                reads=(f"st{par}_{g}", f"mv{par}"), writes=(f"mv{par}",))
        p.step("scalar", lambda e, par=par: e.activation(
            rsqc[:, par], mvar[:, par, :, 1], AF.Sqrt, bias=epsc[:]),
            reads=(f"mv{par}", "epsc", f"rsq{par}"), writes=(f"rsq{par}",))
        p.step("vector", lambda e, par=par: e.reciprocal(rsqc[:, par], rsqc[:, par]),
               reads=(f"rsq{par}",), writes=(f"rsq{par}",))
        p.step("vector", lambda e, par=par: e.tensor_tensor(
            betac[:, par], mvar[:, par, :, 0], rsqc[:, par], ALU.mult),
            reads=(f"rsq{par}", f"mv{par}", f"bet{par}"), writes=(f"bet{par}",))
        p.step("vector", lambda e, par=par: e.tensor_scalar(
            betac[:, par], betac[:, par], -1.0, None, ALU.mult),
            reads=(f"bet{par}",), writes=(f"bet{par}",))
        p.step("vector", lambda e: e.tensor_copy(xt_sb[:, :, 0], cols["xtb"][:]),
               reads=("xtb", "xt_c0"), writes=("xt_c0",))

    def p1_head_xt(s):
        par = s % 2
        for g in range(ngrp):
            p.step("scalar", lambda e, g=g, par=par: e.activation(
                xstd[:, g % 2, :], x_sb[:, g, :], AF.Identity,
                bias=betac[:, par, g:g + 1], scale=rsqc[:, par, g:g + 1]),
                reads=("xs", f"rsq{par}", f"bet{par}", f"xstd{g % 2}"),
                writes=(f"xstd{g % 2}",))
            pk = PTK[g % 2]
            p.step("tensor", lambda e, g=g: [
                e.transpose(pt[g % 2][:, c2 * 128:(c2 + 1) * 128],
                            xstd[:, g % 2, c2 * 128:(c2 + 1) * 128], ident[:])
                for c2 in range(CC)][-1],
                reads=(f"xstd{g % 2}", "ident"), writes=(pk,))
            p.step("scalar", lambda e, g=g: e.activation(
                xt_sb[:, :, g * 128 + 1:g * 128 + 129],
                pt[g % 2][:].rearrange("q (c t) -> q c t", c=CC), AF.Identity),
                reads=(pk, f"xt{g}"), writes=(f"xt{g}",))
        xt_keys = tuple(f"xt{g}" for g in range(ngrp)) + ("xt_c0",)
        p.step("vector", lambda e: e.tensor_copy(cols["xtb"][:], xt_sb[:, :, tslab]),
               reads=xt_keys + ("xtb",), writes=("xtb",))

    def p1_head_mix(s, h):
        par = s % 2
        if h == 0:
            xt_keys = ("xt_c0", "xt0", "xt1")
        else:
            xt_keys = ("xt1", "xt2", "xt3")
        if True:
            off = h * thalf
            for nm, buf, bk in (("mk", xk_sb, "xk"), ("mv", xv_sb, "xv"),
                                ("mr", xq_sb, "xq")):
                p.step("scalar", lambda e, nm=nm, off=off: [
                    e.activation(mixt[:, c2, :], xt_sb[:, c2, off + 1:off + thalf + 1],
                                 AF.Identity, bias=cols["lnb"][:, c2:c2 + 1],
                                 scale=cols[nm + "_a"][:, c2:c2 + 1])
                    for c2 in range(CC)][-1],
                    reads=xt_keys + (nm + "_a", "lnb", "mixt"), writes=("mixt",))
                p.step("vector", lambda e, nm=nm, buf=buf, h=h, off=off: [
                    e.scalar_tensor_tensor(buf[:, h, c2, :],
                                           xt_sb[:, c2, off:off + thalf],
                                           cols[nm + "_b"][:, c2:c2 + 1],
                                           mixt[:, c2, :], ALU.mult, ALU.add)
                    for c2 in range(CC)][-1],
                    reads=xt_keys + (nm + "_b", "mixt", f"{bk}{h}"),
                    writes=(f"{bk}{h}",))

    def p1_mid_dc(s, h, dc):
        par = s % 2
        if True:
            off = h * thalf
            if True:
                pr, kr = (pp[0], "pp0") if dc % 2 == 0 else (pp[3], "pp3")
                p.step("tensor", lambda e, dc=dc, h=h, pr=pr: [
                    e.matmul(pr[:, :thalf], wt["r"][:, c2, dc * 128:(dc + 1) * 128],
                             xq_sb[:, h, c2, :], start=(c2 == 0), stop=(c2 == CC - 1))
                    for c2 in range(CC)][-1],
                    reads=(f"xq{h}", f"wtr{dc}"), writes=(kr,))
                p.step("scalar", lambda e, dc=dc, pr=pr, off=off: e.activation(
                    r_sb[:, dc, off:off + thalf], pr[:, :thalf], AF.Sigmoid),
                    reads=(kr, f"r{dc}"), writes=(f"r{dc}",))
                p.step("tensor", lambda e, dc=dc, h=h: [
                    e.matmul(pp[1][:, :thalf], wt["v"][:, c2, dc * 128:(dc + 1) * 128],
                             xv_sb[:, h, c2, :], start=(c2 == 0), stop=(c2 == CC - 1))
                    for c2 in range(CC)][-1],
                    reads=(f"xv{h}", f"wtv{dc}"), writes=("pp1",))
                p.step("scalar", lambda e, dc=dc, off=off: e.activation(
                    rv_sb[:, dc, off:off + thalf], pp[1][:, :thalf], AF.Identity),
                    reads=("pp1", f"rv_{dc}"), writes=(f"rv_{dc}",))
                p.step("tensor", lambda e, dc=dc, h=h: [
                    e.matmul(pp[2][:, :thalf], wt["k"][:, c2, dc * 128:(dc + 1) * 128],
                             xk_sb[:, h, c2, :], start=(c2 == 0), stop=(c2 == CC - 1))
                    for c2 in range(CC)][-1],
                    reads=(f"xk{h}", f"wtk{dc}"), writes=("pp2",))
                p.step("scalar", lambda e, dc=dc, par=par, off=off: e.activation(
                    k_sb[:, par, dc, off:off + thalf], pp[2][:, :thalf], AF.Identity),
                    reads=("pp2", f"k{par}_{dc}"), writes=(f"k{par}_{dc}",))

    def p1_mid(s, h):
        for dc in range(CC):
            p1_mid_dc(s, h, dc)

    def p1_tail(s):
        par = s % 2
        for c2 in range(CC):
            p2c = c2 % 2
            p.step("vector", lambda e, c2=c2, par=par, p2c=p2c: e.tensor_tensor_scan(
                s_sb[:, p2c, :], cols["decay"][:, c2:c2 + 1].broadcast_to([128, tslab]),
                k_sb[:, par, c2, :], cols["carry"][:, c2:c2 + 1], ALU.mult, ALU.add),
                reads=(f"k{par}_{c2}", "decay", "carry", f"s{p2c}"), writes=(f"s{p2c}",))
            p.step("vector", lambda e, c2=c2, p2c=p2c: e.tensor_copy(
                cols["carry"][:, c2:c2 + 1], s_sb[:, p2c, tslab - 1:tslab]),
                reads=(f"s{p2c}", "carry"), writes=("carry",))
            p.step("vector", lambda e, c2=c2, par=par, p2c=p2c: e.scalar_tensor_tensor(
                t1_sb[:, c2, :], s_sb[:, p2c, :], cols["first"][:, c2:c2 + 1],
                k_sb[:, par, c2, :], ALU.mult, ALU.add),
                reads=(f"s{p2c}", "first", f"k{par}_{c2}", f"t1_{c2}"),
                writes=(f"t1_{c2}",))
        for scr, buf, rkeys, ch in (
                (sc_t1, t1_sb, tuple(f"t1_{c}" for c in range(CC)), "spA"),
                (sc_rv, rv_sb, tuple(f"rv_{c}" for c in range(CC)), "spB"),
                (sc_r2, r_sb, tuple(f"r{c}" for c in range(CC)), "spC")):
            p.dma("gpsimd", f"{ch}{par}",
                  [(scr[:, :, s * tslab:(s + 1) * tslab].rearrange("c q t -> q c t"),
                    buf[:])],
                  reads=rkeys, writes=(f"spill{s}{ch}",))

    p1_xload(0)
    emit_down_scales()
    p1_head_ln(0)
    p1_head_xt(0)
    p1_head_mix(0, 0)
    pprep = 0
    for dc in range(CC):
        for nm in ("r", "v", "k"):
            prep_attn_chunk(nm, w_in["W" + nm], wt[nm], "wt" + nm, dc, pprep % 2)
            pprep += 1
        p1_mid_dc(0, 0, dc)
    p1_head_mix(0, 1)
    if nslab > 1:
        p1_xload(1)
    for s in range(nslab):
        if s + 1 < nslab:
            p1_head_ln(s + 1)
        if s > 0:
            p1_mid(s, 0)
        if s + 1 < nslab:
            p1_head_xt(s + 1)
        drain_wq(8)
        if s + 1 < nslab:
            p1_head_mix(s + 1, 0)
        p1_mid(s, 1)
        if s + 1 < nslab:
            p1_head_mix(s + 1, 1)
        p1_tail(s)
        if s + 2 < nslab:
            p1_xload(s + 2)
    drain_wq(len(wq_units))

    scmv = wq_v[:, 0:8 * nslot_gu].rearrange("q (r j) -> q r j", j=nslot_gu)
    for j in range(nslot_gu):
        p.step("vector", lambda e, j=j: e.tensor_scalar(
            scmv[:, :, j], selfhot[:], scol_gu[:, j:j + 1], None, ALU.mult),
            reads=(f"scg{j}", "selfhot", "wq_v"), writes=("wq_v",))
    p.dma("sync", "scm", [(sc_scm_i[:], wq_v[:, 0:8 * nslot_gu])],
          reads=("wq_v",), writes=("scmi",))

    p.step("vector", lambda e: [e.tensor_scalar(fmask[:, r2, :], cols["carry"][:],
                                                selfhot[:, r2:r2 + 1], None, ALU.mult)
                                for r2 in range(8)][-1],
           reads=("carry", "selfhot"), writes=("fmask",))
    p.dma("sync", "cc1", [(cc_c_i[:], fmask[:].rearrange("q r c -> q (r c)"))],
          reads=("fmask",), writes=("ccci",))

    p.barrier()
    p.step("gpsimd", lambda e: e.collective_compute(
        "AllReduce", ALU.add, RG, [cc_c_i[:]], [cc_c_o[:]]),
        reads=("ccci",), writes=("ccco",))
    p.step("gpsimd", lambda e: e.collective_compute(
        "AllReduce", ALU.add, RG, [sc_scm_i[:]], [sc_scm_o[:]]),
        reads=("scmi",), writes=("scmo",))
    emit_wq_allgathers()
    p.dma("sync", "scr", [(scall[:].rearrange("q r j -> q (r j)"), sc_scm_o[:])],
          reads=("scmo", "scall"), writes=("scall",))
    sb_free(*P1_NAMES)
    h1 = sb("h1", [128, ngq, C], BF16)
    xr_sb = sb("xr_sb", [128, 2, ngrp, C], F32)
    t1r = sb("t1r", [128, 2, CC, tslab], BF16)
    rvr = sb("rvr", [128, 2, CC, tslab], BF16)
    rrr = sb("rrr", [128, 2, CC, tslab], BF16)
    dpow = sb("dpow", [128, CC, tslab], BF16)
    rwk = sb("rwk", [128, 2, CC, tslab], BF16)
    y_sb = sb("y_sb", [128, 2, CC, tslab], BF16)
    h1p = sb("h1p", [128, ngrp, C], F32)
    P2_NAMES = ("xr_sb", "t1r", "rvr", "rrr", "dpow", "rwk", "y_sb", "h1p")
    p.step("vector", lambda e: e.memset(amax1[:], 0.0), writes=("amax1",))

    def cc1_finish():
        p.dma("sync", "cc1", [(fall[:].rearrange("q r c -> q (r c)"), cc_c_o[:])],
              reads=("ccco",), writes=("fall",))
        p.step("vector", lambda e: e.tensor_scalar(cols["carry"][:], fall[:, 0, :],
                                                   csel[:, 0:1], None, ALU.mult),
               reads=("fall", "csel", "carry"), writes=("carry",))
        for r2 in range(1, 8):
            p.step("vector", lambda e, r2=r2: e.scalar_tensor_tensor(
                cols["carry"][:], fall[:, r2, :], csel[:, r2:r2 + 1],
                cols["carry"][:], ALU.mult, ALU.add),
                reads=("fall", "csel", "carry"), writes=("carry",))
        p.step("vector", lambda e: e.tensor_tensor(cols["cf"][:], cols["carry"][:],
                                                   cols["first"][:], ALU.mult),
               reads=("carry", "first"), writes=("cf",))

    def p2_restream(s):
        par = s % 2
        for buf, scr, tg, ch, sch in ((t1r, sc_t1, "t1r", "rsA", "spA"),
                                      (rvr, sc_rv, "rvr", "rsB", "spB"),
                                      (rrr, sc_r2, "rrr", "rsC", "spC")):
            p.dma("scalar", f"{ch}{par}",
                  [(buf[:, par], scr[:, :, s * tslab:(s + 1) * tslab]
                    .rearrange("c q t -> q c t"))],
                  reads=(f"spill{s}{sch}", f"{tg}{par}"), writes=(f"{tg}{par}",))

    def p2_comp(s):
        par = s % 2
        p.dma("sync", f"xr{par}", [(xr_sb[:, par], x_in[s * tslab:(s + 1) * tslab, :]
                                    .rearrange("(g q) c -> q g c", q=128))],
              reads=(f"xr{par}",), writes=(f"xr{par}",))
        if s > 0:
            p.step("vector", lambda e: e.tensor_scalar(
                iota_sl[:], iota_sl[:], float(tslab), None, ALU.add),
                reads=("iota_sl",), writes=("iota_sl",))
        p.step("scalar", lambda e: [
            e.activation(dpow[:, c2, :], iota_sl[:], AF.Exp,
                         scale=cols["lnD"][:, c2:c2 + 1]) for c2 in range(CC)][-1],
            reads=("iota_sl", "lnD", "dpow"), writes=("dpow",))
        p.step("vector", lambda e, par=par: [
            e.scalar_tensor_tensor(rwk[:, par, c2, :], dpow[:, c2, :],
                                   cols["cf"][:, c2:c2 + 1], t1r[:, par, c2, :],
                                   ALU.mult, ALU.add) for c2 in range(CC)][-1],
            reads=("dpow", "cf", f"t1r{par}", f"rwk{par}"), writes=(f"rwk{par}",))
        p.step("vector", lambda e, par=par: [
            e.tensor_tensor(rwk[:, par, c2, :], rwk[:, par, c2, :],
                            rvr[:, par, c2, :], ALU.mult) for c2 in range(CC)][-1],
            reads=(f"rwk{par}", f"rvr{par}"), writes=(f"rwk{par}",))
        p.step("vector", lambda e, par=par: [
            e.tensor_tensor(rwk[:, par, c2, :], rwk[:, par, c2, :],
                            rrr[:, par, c2, :], ALU.mult) for c2 in range(CC)][-1],
            reads=(f"rwk{par}", f"rrr{par}"), writes=(f"rwk{par}",))

    def p2_mid(s):
        par = s % 2
        for dc in range(CC):
            bi = dc % 2
            p.step("tensor", lambda e, dc=dc, bi=bi, par=par: [
                e.matmul(pp[bi][:, :tslab],
                         wto[:, c2, dc * 128:(dc + 1) * 128], rwk[:, par, c2, :],
                         start=(c2 == 0), stop=(c2 == CC - 1)) for c2 in range(CC)][-1],
                reads=(f"rwk{par}", "wto"), writes=(f"pp{bi}",))
            p.step("scalar", lambda e, dc=dc, bi=bi, par=par: e.activation(
                y_sb[:, par, dc, :], pp[bi][:, :tslab], AF.Identity),
                reads=(f"pp{bi}", f"y{par}_{dc}"), writes=(f"y{par}_{dc}",))

    def p2_tail(s):
        par = s % 2
        ykeys = tuple(f"y{par}_{d}" for d in range(CC))
        for g in range(ngrp):
            pk = PTK[g % 2]
            p.step("tensor", lambda e, g=g, par=par: [
                e.transpose(pt[g % 2][:, c2 * 128:(c2 + 1) * 128],
                            y_sb[:, par, c2, g * 128:(g + 1) * 128], ident[:])
                for c2 in range(CC)][-1],
                reads=ykeys + ("ident",), writes=(pk,))
            p.step("vector", lambda e, g=g, par=par: e.scalar_tensor_tensor(
                h1p[:, g, :], xr_sb[:, par, g, :], 2.0, pt[g % 2][:],
                ALU.mult, ALU.add),
                reads=(pk, f"xr{par}", f"h1p{g}"), writes=(f"h1p{g}",))
            p.step("vector", lambda e, g=g, par=par: [
                e.bn_stats(stats[:, par, g, 0, :], h1p[:, g, 0:512]),
                e.bn_stats(stats[:, par, g, 1, :], h1p[:, g, 512:1024])][-1],
                reads=(f"h1p{g}", f"st{par}_{g}"), writes=(f"st{par}_{g}",))
            p.step("vector", lambda e, g=g, par=par: e.bn_aggr(
                mvar[:, par, g, :], stats[:, par, g, :, :]),
                reads=(f"st{par}_{g}", f"mv{par}"), writes=(f"mv{par}",))
        p.step("scalar", lambda e, par=par: e.activation(
            rsqc[:, par], mvar[:, par, :, 1], AF.Sqrt, bias=epsc[:]),
            reads=(f"mv{par}", "epsc", f"rsq{par}"), writes=(f"rsq{par}",))
        p.step("vector", lambda e, par=par: e.reciprocal(rsqc[:, par], rsqc[:, par]),
               reads=(f"rsq{par}",), writes=(f"rsq{par}",))
        p.step("vector", lambda e, par=par: e.tensor_tensor(
            betac[:, par], mvar[:, par, :, 0], rsqc[:, par], ALU.mult),
            reads=(f"rsq{par}", f"mv{par}", f"bet{par}"), writes=(f"bet{par}",))
        p.step("vector", lambda e, par=par: e.tensor_scalar(
            betac[:, par], betac[:, par], -1.0, None, ALU.mult),
            reads=(f"bet{par}",), writes=(f"bet{par}",))
        for g in range(ngrp):
            gg = s * ngrp + g
            p.step("scalar", lambda e, g=g, gg=gg, par=par: e.activation(
                h1[:, gg, :], h1p[:, g, :], AF.Identity,
                bias=betac[:, par, g:g + 1], scale=rsqc[:, par, g:g + 1]),
                reads=(f"h1p{g}", f"rsq{par}", f"bet{par}"), writes=(f"h1_{gg}",))
            p.step("vector", lambda e, gg=gg: e.tensor_reduce(
                scal[:, 8:9], h1[:, gg, :], AXX, ALU.max, apply_absolute_value=True),
                reads=(f"h1_{gg}", "amt"), writes=("amt",))
            p.step("vector", lambda e: e.tensor_max(amax1[:], amax1[:], scal[:, 8:9]),
                   reads=("amt", "amax1"), writes=("amax1",))

    p2_restream(0)
    if nslab > 1:
        p2_restream(1)
    cc1_finish()
    p2_comp(0)
    for s in range(nslab):
        p2_mid(s)
        if s + 2 < nslab:
            p2_restream(s + 2)
        if s + 1 < nslab:
            p2_comp(s + 1)
        p2_tail(s)

    def cc_absmax(amax, cin, cout, key, ch, hr):
        p.step("vector", lambda e: e.tensor_copy(abf[:], amax[:]),
               reads=(key, "abf"), writes=("abf",))
        p.step("tensor", lambda e: e.transpose(pt[0][0:1, 0:128], abf[:], ident[:]),
               reads=("abf", "ident", PTK[0]), writes=(PTK[0],))
        p.step("vector", lambda e: e.tensor_reduce(amsc[:], pt[0][0:1, 0:128],
                                                   AXX, ALU.max),
               reads=(PTK[0], "amsc"), writes=("amsc",))
        p.dma("sync", ch, [(cin[:], amsc[:])], reads=("amsc",), writes=(key + "i",))
        p.step("gpsimd", lambda e: e.collective_compute(
            "AllReduce", ALU.max, RG, [cin[:]], [cout[:]]),
            reads=(key + "i",), writes=(key + "o",))
        p.dma("sync", ch, [(amsc[:], cout[:])], reads=(key + "o", "amsc"),
              writes=("amsc",))
        p.step("vector", lambda e: e.tensor_copy(amsb[:], amsc[:]),
               reads=("amsc", "amsb"), writes=("amsb",))
        p.step("tensor", lambda e: e.matmul(pp[3][:, 0:1], ones_row[:], amsb[:],
                                            start=True, stop=True),
               reads=("amsb", "ones_row", "pp3"), writes=("pp3",))
        p.step("vector", lambda e: e.tensor_copy(amax[:], pp[3][:, 0:1]),
               reads=("pp3", key), writes=(key,))

    cc_absmax(amax1, cc_m_i[0], cc_m_o[0], "amax1", "ccm1", hrow[0])
    p.step("scalar", lambda e: e.activation(scal[:, 6:7], clipx[:], AF.Sigmoid),
           reads=("clipx",), writes=("sgx",))
    p.step("vector", lambda e: e.tensor_scalar(scal[:, 1:2], amax1[:], scal[:, 6:7],
                                               1.0 / QMAX, ALU.mult, ALU.mult),
           reads=("amax1", "sgx"), writes=("scale1",))
    p.step("vector", lambda e: e.reciprocal(scal[:, 2:3], scal[:, 1:2]),
           reads=("scale1",), writes=("inv1",))

    p.barrier()
    sb_free(*P2_NAMES)
    xq1_fm = sb("xq1_fm", [128, CC, tcore], BF16)
    q_v = sb("q_v", [128, 2, C], F32)
    q_i = sb("q_i", [128, C], mybir.dt.int32)
    xq1_tm = sb("xq1_tm", [128, C], BF16)
    wgT = sb("wgT", [128, 2, CC, 128], BF16)
    wuT = sb("wuT", [128, 2, CC, 128], BF16)
    sg_sb = sb("sg_sb", [128, 2, tcore], BF16)
    h2_sb = sb("h2_sb", [128, 2, tcore], F32)
    sccol = sb("sccol", [128, 4], F32)
    P3_NAMES = ("xq1_fm", "q_v", "q_i", "xq1_tm", "wgT", "wuT", "sg_sb", "h2_sb",
                "sccol")

    for gg in range(ngq):
        par = gg % 2
        p.step("vector", lambda e, gg=gg, par=par: e.tensor_scalar(
            q_v[:, par, :], h1[:, gg, :], scal[:, 2:3], QMAX, ALU.mult, ALU.min),
            reads=(f"h1_{gg}", "inv1", f"qv{par}"), writes=(f"qv{par}",))
        p.step("vector", lambda e, par=par: e.tensor_scalar(
            q_i[:], q_v[:, par, :], -QMAX - 1.0, None, ALU.max),
            reads=(f"qv{par}", "qi"), writes=("qi",))
        p.step("scalar", lambda e: e.activation(xq1_tm[:], q_i[:], AF.Identity),
               reads=("qi", "xq1tm"), writes=("xq1tm",))
        pk = PTK[par]
        p.step("tensor", lambda e, par=par: [
            e.transpose(pt[par][:, c2 * 128:(c2 + 1) * 128],
                        xq1_tm[:, c2 * 128:(c2 + 1) * 128], ident[:])
            for c2 in range(CC)][-1],
            reads=("xq1tm", "ident"), writes=(pk,))
        p.step("scalar", lambda e, gg=gg, par=par: e.activation(
            xq1_fm[:, :, gg * 128:(gg + 1) * 128],
            pt[par][:].rearrange("q (c t) -> q c t", c=CC), AF.Identity),
            reads=(pk,), writes=(f"xq1f{gg}",))
    xq1_keys = tuple(f"xq1f{gg}" for gg in range(ngq))

    p.step("vector", lambda e: e.memset(amax2[:], 0.0), writes=("amax2",))
    ic_n_ = ic_n
    ntB = max(1, tcore // 512)
    tB = min(512, tcore)
    for ic in range(ic_n_):
        par = ic % 2
        gg_, gu_ = ic, ic_n + ic
        p.dma("sync", f"wb{par}",
              [(wgT[:, par].rearrange("q c d -> q (c d)"),
                sc_gu_all[gg_ % 8, gg_ // 8, :, :]),
               (wuT[:, par].rearrange("q c d -> q (c d)"),
                sc_gu_all[gu_ % 8, gu_ // 8, :, :])],
              reads=("ag_gu", f"wgT{par}", f"wuT{par}"),
              writes=(f"wgT{par}", f"wuT{par}"))
        p.step("vector", lambda e, par=par, gg_=gg_, gu_=gu_: [
            e.tensor_tensor(sccol[:, par * 2:par * 2 + 1],
                            scall[:, gg_ % 8, gg_ // 8:gg_ // 8 + 1],
                            scal[:, 1:2], ALU.mult),
            e.tensor_tensor(sccol[:, par * 2 + 1:par * 2 + 2],
                            scall[:, gu_ % 8, gu_ // 8:gu_ // 8 + 1],
                            scal[:, 1:2], ALU.mult)][-1],
            reads=("scall", "scale1", f"scc{par}"), writes=(f"scc{par}",))
        for tti in range(ntB):
            bg, bu = 2 * (tti % 2), 2 * (tti % 2) + 1
            p.step("tensor", lambda e, par=par, tti=tti, bg=bg: [
                e.matmul(pp[bg][:, :tB], wgT[:, par, c2, :],
                         xq1_fm[:, c2, tti * tB:(tti + 1) * tB],
                         start=(c2 == 0), stop=(c2 == CC - 1)) for c2 in range(CC)][-1],
                reads=(f"wgT{par}",) + xq1_keys, writes=(f"pp{bg}",))
            p.step("scalar", lambda e, par=par, tti=tti, bg=bg: e.activation(
                sg_sb[:, par, tti * tB:(tti + 1) * tB], pp[bg][:, :tB], AF.Silu,
                scale=sccol[:, par * 2:par * 2 + 1]),
                reads=(f"pp{bg}", f"scc{par}", f"sg{par}"), writes=(f"sg{par}",))
            p.step("tensor", lambda e, par=par, tti=tti, bu=bu: [
                e.matmul(pp[bu][:, :tB], wuT[:, par, c2, :],
                         xq1_fm[:, c2, tti * tB:(tti + 1) * tB],
                         start=(c2 == 0), stop=(c2 == CC - 1)) for c2 in range(CC)][-1],
                reads=(f"wuT{par}",) + xq1_keys, writes=(f"pp{bu}",))
            p.step("vector", lambda e, par=par, tti=tti, bu=bu: e.scalar_tensor_tensor(
                h2_sb[:, par, tti * tB:(tti + 1) * tB], pp[bu][:, :tB],
                sccol[:, par * 2 + 1:par * 2 + 2], sg_sb[:, par, tti * tB:(tti + 1) * tB],
                ALU.mult, ALU.mult),
                reads=(f"pp{bu}", f"sg{par}", f"scc{par}", f"h2{par}"),
                writes=(f"h2{par}",))
            p.step("vector", lambda e, par=par, tti=tti: e.tensor_reduce(
                scal[:, 9:10], h2_sb[:, par, tti * tB:(tti + 1) * tB], AXX, ALU.max,
                apply_absolute_value=True),
                reads=(f"h2{par}", "am2t"), writes=("am2t",))
            p.step("vector", lambda e: e.tensor_max(amax2[:], amax2[:], scal[:, 9:10]),
                   reads=("am2t", "amax2"), writes=("amax2",))
        p.dma("gpsimd", f"h2o{par}", [(sc_h2[ic, :, :], h2_sb[:, par, :])],
              reads=(f"h2{par}",), writes=(f"sch2_{ic}",))

    cc_absmax(amax2, cc_m_i[1], cc_m_o[1], "amax2", "ccm2", hrow[1])
    p.step("scalar", lambda e: e.activation(scal[:, 7:8], clipd[:], AF.Sigmoid),
           reads=("clipd",), writes=("sgd",))
    p.step("vector", lambda e: e.tensor_scalar(scal[:, 4:5], amax2[:], scal[:, 7:8],
                                               1.0 / QMAX, ALU.mult, ALU.mult),
           reads=("amax2", "sgd"), writes=("scale2",))
    p.step("vector", lambda e: e.reciprocal(scal[:, 5:6], scal[:, 4:5]),
           reads=("scale2",), writes=("inv2",))
    p.step("vector", lambda e: e.tensor_scalar(swd[:], swd[:], scal[:, 4:5], None,
                                               ALU.mult),
           reads=("scale2", "wsc_d"), writes=("swd2",))

    p.barrier()
    sb_free(*P3_NAMES)
    wdT = sb("wdT", [128, ic_n, C], BF16)
    v2i = sb("v2i", [128, 2, ic_n, thalf], BF16)
    v2f = sb("v2f", [128, (ic_n + 1) // 2, thalf], F32)
    f2i = sb("f2i", [128, (ic_n + 1) // 2, thalf], mybir.dt.int32)
    xq2 = sb("xq2", [128, 2, ic_n, thalf], BF16)
    dn_sb = sb("dn_sb", [128, 2, CC, thalf], BF16)
    o_pre = sb("o_pre", [128, thalf // 128, C], F32)
    o_sb = sb("o_sb", [128, 2, thalf // 128, C], F32)
    ostats = sb("ostats", [128, thalf // 128, 2, 6], F32)
    omv = sb("omv", [128, thalf // 128, 2], F32)
    orsq = sb("orsq", [128, thalf // 128], F32)
    obeta = sb("obeta", [128, thalf // 128], F32)
    P4_NAMES = ("wdT", "v2i", "v2f", "f2i", "xq2", "dn_sb", "o_pre", "o_sb", "ostats", "omv", "orsq", "obeta")

    if dbg:
        p.dma("sync", "dbgd",
              [(dbg_h1[:].rearrange("(g q) c -> q g c", q=128), h1[:])] +
              [(dbg_m[:, 0:1], amax1[0:1, :]), (dbg_m[:, 1:2], amax2[0:1, :]),
               (dbg_m[:, 2:3], scal[0:1, 2:3]), (dbg_m[:, 3:4], scal[0:1, 5:6])],
              reads=tuple(f"h1_{gg}" for gg in range(ngq)) + ("amax1", "amax2", "inv1", "inv2"),
              writes=("dbgout",))
    half_ic = (ic_n + 1) // 2
    p.dma("sync", "wdl",
          [(wdT[:, ic2, :], sc_d_all[ic2 % 8, ic2 // 8, :, :])
           for ic2 in range(ic_n)],
          reads=("ag_d",), writes=("wdT",))
    nch = tcore // thalf
    og = thalf // 128

    def p4_head(hs):
        par = hs % 2
        p.dma("gpsimd", f"h2i{par}",
              [(v2i[:, par, :half_ic, :],
                sc_h2[:half_ic, :, hs * thalf:(hs + 1) * thalf]
                .rearrange("i q t -> q i t")),
               (v2i[:, par, half_ic:, :],
                sc_h2[half_ic:, :, hs * thalf:(hs + 1) * thalf]
                .rearrange("i q t -> q i t"))],
              reads=tuple(f"sch2_{ic}" for ic in range(ic_n)) + (f"v2i{par}",),
              writes=(f"v2i{par}",))
        for sub in range(2):
            i0 = sub * half_ic
            i1 = min(ic_n, i0 + half_ic)
            ni = i1 - i0
            if ni <= 0:
                continue
            p.step("vector", lambda e, par=par, i0=i0, ni=ni: e.tensor_scalar(
                v2f[:, :ni, :].rearrange("q i t -> q (i t)"),
                v2i[:, par, i0:i0 + ni, :].rearrange("q i t -> q (i t)"),
                scal[:, 5:6], QMAX, ALU.mult, ALU.min),
                reads=(f"v2i{par}", "inv2", "v2f"), writes=("v2f",))
            p.step("vector", lambda e, ni=ni: e.tensor_scalar(
                f2i[:, :ni, :].rearrange("q i t -> q (i t)"),
                v2f[:, :ni, :].rearrange("q i t -> q (i t)"), -QMAX - 1.0, None,
                ALU.max),
                reads=("v2f", "f2i"), writes=("f2i",))
            p.step("vector", lambda e, par=par, i0=i0, ni=ni: e.tensor_copy(
                xq2[:, par, i0:i0 + ni, :].rearrange("q i t -> q (i t)"),
                f2i[:, :ni, :].rearrange("q i t -> q (i t)")),
                reads=("f2i", f"xq2{par}"), writes=(f"xq2{par}",))

    def p4_mid(hs):
        par = hs % 2
        for hc in range(CC):
            pd = pp[hc % 4][:, :thalf]
            kd = f"pp{hc % 4}"
            p.step("tensor", lambda e, hc=hc, pd=pd, par=par: [
                e.matmul(pd, wdT[:, i2, hc * 128:(hc + 1) * 128],
                         xq2[:, par, i2, :], start=(i2 == 0), stop=(i2 == ic_n - 1))
                for i2 in range(ic_n)][-1],
                reads=(f"xq2{par}", "wdT"), writes=(kd,))
            p.step("scalar", lambda e, hc=hc, pd=pd, par=par: e.activation(
                dn_sb[:, par, hc, :], pd, AF.Identity, scale=swd[:, hc:hc + 1]),
                reads=(kd, "swd2", f"dn{par}_{hc}"), writes=(f"dn{par}_{hc}",))

    def p4_tail(hs):
        par = hs % 2
        dnkeys = tuple(f"dn{par}_{d}" for d in range(CC))
        for g in range(og):
            gg = hs * og + g
            pk = PTK[g % 2]
            p.step("tensor", lambda e, g=g, par=par: [
                e.transpose(pt[g % 2][:, c2 * 128:(c2 + 1) * 128],
                            dn_sb[:, par, c2, g * 128:(g + 1) * 128], ident[:])
                for c2 in range(CC)][-1],
                reads=dnkeys + ("ident",), writes=(pk,))
            p.step("vector", lambda e, g=g, gg=gg: e.tensor_tensor(
                o_pre[:, g, :], h1[:, gg, :], pt[g % 2][:], ALU.add),
                reads=(pk, f"h1_{gg}", f"opre_{g}"), writes=(f"opre_{g}",))
            p.step("vector", lambda e, g=g: [
                e.bn_stats(ostats[:, g, 0, :], o_pre[:, g, 0:512]),
                e.bn_stats(ostats[:, g, 1, :], o_pre[:, g, 512:1024])][-1],
                reads=(f"opre_{g}", f"ost{g}"), writes=(f"ost{g}",))
            p.step("vector", lambda e, g=g: e.bn_aggr(omv[:, g, :], ostats[:, g, :, :]),
                   reads=(f"ost{g}", "omv"), writes=("omv",))
        p.step("scalar", lambda e: e.activation(orsq[:], omv[:, :, 1], AF.Sqrt,
                                                bias=epsc[:]),
               reads=("omv", "epsc", "orsq"), writes=("orsq",))
        p.step("vector", lambda e: e.reciprocal(orsq[:], orsq[:]),
               reads=("orsq",), writes=("orsq",))
        p.step("vector", lambda e: e.tensor_tensor(obeta[:], omv[:, :, 0], orsq[:],
                                                   ALU.mult),
               reads=("orsq", "omv", "obeta"), writes=("obeta",))
        p.step("vector", lambda e: e.tensor_scalar(obeta[:], obeta[:], -1.0, None,
                                                   ALU.mult),
               reads=("obeta",), writes=("obeta",))
        for g in range(og):
            p.step("scalar", lambda e, g=g, par=par: e.activation(
                o_sb[:, par, g, :], o_pre[:, g, :], AF.Identity,
                bias=obeta[:, g:g + 1], scale=orsq[:, g:g + 1]),
                reads=(f"opre_{g}", "orsq", "obeta", f"o{par}"), writes=(f"o{par}",))
        p.dma("sync", f"out{par}",
              [(out_hbm[hs * thalf:(hs + 1) * thalf, :]
                .rearrange("(g q) c -> q g c", q=128), o_sb[:, par])],
              reads=(f"o{par}",), writes=(f"outd{hs}",))

    p4_head(0)
    for hs in range(nch):
        p4_mid(hs)
        if hs + 1 < nch:
            p4_head(hs + 1)
        p4_tail(hs)

    fin = dict(p.dma_tot)
    p.eng_cnt["sync"] += 1
    p.steps.append(("sync", fin, None, "E_sync", False))

    with nc.Block() as block:
        semstack = p.emit_all(nc, block)
    sb_free(*P4_NAMES)
    for name in reversed(list(open_t)):
        open_t.pop(name).__exit__(None, None, None)
    semstack.close()
    return nc


def _wq_slices(wg, wu, wd):
    ic_n = wg.shape[0] // 128
    ngu = 2 * ic_n
    nslot_gu = (ngu + 7) // 8
    nslot_d = (ic_n + 7) // 8
    gus, wds, wdsc = [], [], []
    for r in range(8):
        rows = []
        for j in range(nslot_gu):
            g = r + 8 * j
            if g >= ngu:
                g = r % ngu
            w = wg if g < ic_n else wu
            rc = g if g < ic_n else g - ic_n
            rows.append(w[rc * 128:(rc + 1) * 128, :])
        gus.append(np.ascontiguousarray(np.concatenate(rows, axis=0)))
        colsl = []
        for j in range(nslot_d):
            ic = r + 8 * j
            if ic >= ic_n:
                ic = r % ic_n
            colsl.append(wd[:, ic * 128:(ic + 1) * 128])
        wds.append(np.ascontiguousarray(np.concatenate(colsl, axis=1)))
        wdsc.append(np.ascontiguousarray(wd[r * 128:(r + 1) * 128, :]))
    return gus, wds, wdsc


def _shard_inputs(inputs, tcore):
    x = np.ascontiguousarray(np.asarray(inputs["L0_attn_x"], dtype=np.float32))
    xf = x.reshape(-1, C)
    halves = max(1, (xf.shape[0] // tcore) // max(1, B))
    per_b = xf.shape[0] // B if xf.shape[0] >= B * tcore else tcore
    in_maps = []
    common = {
        "Wk": np.ascontiguousarray(np.asarray(inputs["L0_attn_Wk"], np.float32)),
        "Wv": np.ascontiguousarray(np.asarray(inputs["L0_attn_Wv"], np.float32)),
        "Wr": np.ascontiguousarray(np.asarray(inputs["L0_attn_Wr"], np.float32)),
        "Wo": np.ascontiguousarray(np.asarray(inputs["L0_attn_Wo"], np.float32)),
        "wd": np.ascontiguousarray(np.asarray(inputs["L0_ffn_w_down"], np.float32)),
        "tmk": np.ascontiguousarray(np.asarray(inputs["L0_attn_time_mix_k"], np.float32)),
        "tmv": np.ascontiguousarray(np.asarray(inputs["L0_attn_time_mix_v"], np.float32)),
        "tmr": np.ascontiguousarray(np.asarray(inputs["L0_attn_time_mix_r"], np.float32)),
        "tdecay": np.ascontiguousarray(np.asarray(inputs["L0_attn_time_decay"], np.float32)),
        "tfirst": np.ascontiguousarray(np.asarray(inputs["L0_attn_time_first"], np.float32)),
        "lnw": np.ascontiguousarray(np.asarray(inputs["L0_attn_ln_w"], np.float32)),
        "lnb": np.ascontiguousarray(np.asarray(inputs["L0_attn_ln_b"], np.float32)),
        "clipx": np.full((128, 1), np.float32(np.asarray(inputs["L0_ffn_clip_x"]).ravel()[0]),
                         np.float32),
        "clipd": np.full((128, 1), np.float32(np.asarray(inputs["L0_ffn_clip_down"]).ravel()[0]),
                         np.float32),
    }
    gus, wds, wdsc = _wq_slices(np.asarray(inputs["L0_ffn_w_gate"], np.float32),
                                np.asarray(inputs["L0_ffn_w_up"], np.float32),
                                np.asarray(inputs["L0_ffn_w_down"], np.float32))
    for core in range(NCORES):
        b = core // 2
        h = core % 2
        s = b * T + h * tcore
        xs = np.ascontiguousarray(xf[s:s + tcore])
        if h == 0:
            xb = np.ascontiguousarray(xf[s:s + 1])
            mp = 0.0
        else:
            xb = np.ascontiguousarray(xf[s - 1:s])
            mp = 1.0
        sh = np.zeros((128, 8), np.float32)
        sh[:, core] = 1.0
        cs = np.zeros((128, 8), np.float32)
        if h > 0:
            cs[:, core - 1] = 1.0
        m = dict(common)
        m.update({"x": xs, "xbound": xb,
                  "mprev": np.full((128, 1), mp, np.float32),
                  "selfhot": sh, "csel": cs,
                  "wgu_mine": gus[core], "wd_mine": wds[core],
                  "wdsc_mine": wdsc[core]})
        in_maps.append(m)
    return in_maps


def kernel(**inputs):
    tcore = T * B // NCORES
    in_maps = _shard_inputs(inputs, tcore)
    nc = build_nc(tcore)
    res = run_bass_kernel_spmd(nc, in_maps, core_ids=list(range(NCORES)))
    outs = [np.asarray(r["out"], dtype=np.float32) for r in res.results]
    return np.concatenate(outs, axis=0).reshape(B, T, C)


# version 35
# speedup vs baseline: 1.1871x; 1.1871x over previous
import numpy as np

import concourse.bass as bass
import concourse.mybir as mybir
from concourse.bass_utils import run_bass_kernel_spmd
from concourse.bass_isa import ReduceOp

F32 = mybir.dt.float32
BF16 = mybir.dt.bfloat16
ALU = mybir.AluOpType
AF = mybir.ActivationFunctionType
AXX = mybir.AxisListType.X

B, T, C, I = 4, 4096, 1024, 2816
NCORES = 8
CC = C // 128
QMAX = 127.0
LN_EPS = 1e-5

ENGINES = ("sync", "scalar", "vector", "tensor", "gpsimd")


class Prog:

    def __init__(self):
        self.steps = []
        self.eng_cnt = {e: 0 for e in ENGINES}
        self.last_w = {}
        self.readers = {}
        self.dma_tot = {}
        self.sem_names = set("E_" + e for e in ENGINES)
        self.pending = {e: {} for e in ENGINES}

    def _dep_waits(self, reads, writes):
        waits = {}

        def add(s, v):
            if waits.get(s, 0) < v:
                waits[s] = v

        for k in reads:
            d = self.last_w.get(k)
            if d:
                add(*d)
        for k in writes:
            d = self.last_w.get(k)
            if d:
                add(*d)
            for s, v in self.readers.get(k, {}).items():
                add(s, v)
        return waits

    def step(self, engine, emit, reads=(), writes=()):
        waits = self._dep_waits(reads, writes)
        if engine == "gpsimd" and self.eng_cnt[engine] > 0:
            if waits.get("E_gpsimd", 0) < self.eng_cnt[engine]:
                waits["E_gpsimd"] = self.eng_cnt[engine]
        if self.pending[engine]:
            for s, v in self.pending[engine].items():
                if waits.get(s, 0) < v:
                    waits[s] = v
            self.pending[engine] = {}
        self.eng_cnt[engine] += 1
        val = self.eng_cnt[engine]
        sem = "E_" + engine
        self.steps.append((engine, waits, emit, sem, False))
        for k in writes:
            self.last_w[k] = (sem, val)
            self.readers[k] = {}
        for k in reads:
            self.readers.setdefault(k, {})[sem] = val

    def dma(self, engine, ch, pairs, reads=(), writes=(), slow=False):
        sem = "D_" + ch
        self.sem_names.add(sem)
        waits = self._dep_waits(reads, writes)
        if self.pending[engine]:
            for s_, v_ in self.pending[engine].items():
                if waits.get(s_, 0) < v_:
                    waits[s_] = v_
            self.pending[engine] = {}
        prev = self.dma_tot.get(sem, 0)
        if prev and waits.get(sem, 0) < prev:
            waits[sem] = prev
        tot = prev + 16 * len(pairs)
        self.dma_tot[sem] = tot

        def emit(eng, s, pairs=pairs, slow=slow):
            for o, i in pairs:
                eng.dma_start(out=o, in_=i,
                              allow_slow_non_contiguous=slow).then_inc(s, 16)

        self.steps.append((engine, waits, emit, sem, True))
        for k in writes:
            self.last_w[k] = (sem, tot)
            self.readers[k] = {}
        for k in reads:
            self.readers.setdefault(k, {})[sem] = tot

    def barrier(self):
        snap = {("E_" + e): self.eng_cnt[e] for e in ENGINES if self.eng_cnt[e] > 0}
        snap.update({s: v for s, v in self.dma_tot.items() if v > 0})
        for e in ENGINES:
            w = dict(self.pending[e])
            for s, v in snap.items():
                if w.get(s, 0) < v:
                    w[s] = v
            self.pending[e] = w
        self.last_w = {}
        self.readers = {}

    def emit_all(self, nc, block):
        import contextlib
        stack = contextlib.ExitStack()
        sems = {n: stack.enter_context(nc.semaphore(n)) for n in sorted(self.sem_names)}
        watermark = {e: {} for e in ENGINES}

        def run_engine(eng, ename):
            wm = watermark[ename]
            for engine, waits, emit, sem, is_dma in self.steps:
                if engine != ename:
                    continue
                for s in sorted(waits):
                    v = waits[s]
                    if wm.get(s, 0) < v:
                        eng.wait_ge(sems[s], v)
                        wm[s] = v
                if emit is None:
                    eng.sem_inc(sems[sem], 1)
                elif is_dma:
                    emit(eng, sems[sem])
                else:
                    insts = emit(eng)
                    last = insts[-1] if isinstance(insts, (list, tuple)) else insts
                    last.then_inc(sems[sem], 1)

        for ename in ENGINES:
            getattr(block, ename)(lambda eng, ename=ename: run_engine(eng, ename))
        return stack


def build_nc(tcore=T * B // NCORES, icdim=I, dbg=False):
    ic_n = icdim // 128
    tslab = min(512, tcore)
    nslab = tcore // tslab
    ngrp = tslab // 128
    nhalf = 2 if tslab >= 512 else 1
    thalf = tslab // nhalf
    ngq = tcore // 128

    nc = bass.Bass("TRN2", target_bir_lowering=False)

    x_in = nc.dram_tensor("x", [tcore, C], F32, kind="ExternalInput").ap()
    xb_in = nc.dram_tensor("xbound", [1, C], F32, kind="ExternalInput").ap()
    out_hbm = nc.dram_tensor("out", [tcore, C], F32, kind="ExternalOutput").ap()
    w_in = {nm: nc.dram_tensor(nm, [C, C], F32, kind="ExternalInput").ap()
            for nm in ("Wk", "Wv", "Wr", "Wo")}
    wd_in = nc.dram_tensor("wd", [C, icdim], F32, kind="ExternalInput").ap()
    wgu_in = nc.dram_tensor("wgu_mine", [(2 * ic_n + 7) // 8 * 128, C], F32,
                            kind="ExternalInput").ap()
    wdm_in = nc.dram_tensor("wd_mine", [C, (ic_n + 7) // 8 * 128], F32,
                            kind="ExternalInput").ap()
    wdsc_in = nc.dram_tensor("wdsc_mine", [128, icdim], F32,
                             kind="ExternalInput").ap()
    vec_in = {n: nc.dram_tensor(n, [C], F32, kind="ExternalInput").ap()
              for n in ("tmk", "tmv", "tmr", "tdecay", "tfirst", "lnw", "lnb")}
    mprev_in = nc.dram_tensor("mprev", [128, 1], F32, kind="ExternalInput").ap()
    selfhot_in = nc.dram_tensor("selfhot", [128, 8], F32, kind="ExternalInput").ap()
    csel_in = nc.dram_tensor("csel", [128, 8], F32, kind="ExternalInput").ap()
    clipx_in = nc.dram_tensor("clipx", [128, 1], F32, kind="ExternalInput").ap()
    clipd_in = nc.dram_tensor("clipd", [128, 1], F32, kind="ExternalInput").ap()

    SCRK = "ExternalOutput" if dbg else "Internal"
    ngu = 2 * ic_n
    nslot_gu = (ngu + 7) // 8
    nslot_d = (ic_n + 7) // 8
    sc_t1 = nc.dram_tensor("sc_t1", [CC, 128, tcore], BF16, kind=SCRK).ap()
    sc_rv = nc.dram_tensor("sc_rv", [CC, 128, tcore], BF16, kind=SCRK).ap()
    sc_r2 = nc.dram_tensor("sc_r2", [CC, 128, tcore], BF16, kind="Internal").ap()
    sc_gu_mine = nc.dram_tensor("sc_gu_mine", [nslot_gu, 128, C], BF16,
                                kind="Internal").ap()
    sc_gu_all = nc.dram_tensor("sc_gu_all", [8, nslot_gu, 128, C], BF16,
                               kind="Internal", addr_space="Shared").ap()
    sc_scm_i = nc.dram_tensor("sc_scm_i", [128, 8 * nslot_gu], F32,
                              kind="Internal").ap()
    sc_scm_o = nc.dram_tensor("sc_scm_o", [128, 8 * nslot_gu], F32, kind="Internal",
                              addr_space="Shared").ap()
    sc_d_mine = nc.dram_tensor("sc_d_mine", [nslot_d, 128, C], BF16,
                               kind="Internal").ap()
    sc_sd_mine = nc.dram_tensor("sc_sd_mine", [1, 128], F32, kind="Internal").ap()
    sc_sd_all = nc.dram_tensor("sc_sd_all", [8, 128], F32, kind="Internal",
                               addr_space="Shared").ap()
    sc_d_all = nc.dram_tensor("sc_d_all", [8, nslot_d, 128, C], BF16,
                              kind="Internal", addr_space="Shared").ap()
    sc_h2 = nc.dram_tensor("sc_h2", [ic_n, 128, tcore], BF16, kind=SCRK).ap()
    cc_c_i = nc.dram_tensor("cc_c_i", [128, 8 * CC], F32, kind="Internal").ap()
    cc_c_o = nc.dram_tensor("cc_c_o", [128, 8 * CC], F32, kind="Internal",
                            addr_space="Shared").ap()
    cc_m_i = [nc.dram_tensor(f"cc_m_i{j}", [1, 1], F32, kind="Internal").ap()
              for j in range(2)]
    hrow = [nc.dram_tensor(f"hrow{j}", [1, 128], F32, kind="Internal").ap()
            for j in range(2)]
    cc_m_o = [nc.dram_tensor(f"cc_m_o{j}", [1, 1], F32, kind="Internal",
                             addr_space="Shared").ap() for j in range(2)]
    dbg_h1 = (nc.dram_tensor("dbg_h1", [tcore, C], F32, kind="ExternalOutput").ap()
              if dbg else None)
    dbg_m = (nc.dram_tensor("dbg_m", [1, 4], F32, kind="ExternalOutput").ap()
             if dbg else None)
    RG = [list(range(NCORES))]

    p = Prog()
    open_t = {}

    def sb(name, shape, dt):
        tt = nc.sbuf_tensor("s_" + name, shape, dt)
        h = tt.__enter__()
        open_t[name] = tt
        return h

    def sb_free(*names):
        for name in reversed(names):
            open_t.pop(name).__exit__(None, None, None)

    def ps(name, shape, dt):
        tt = nc.psum_tensor(name, shape, dt)
        h = tt.__enter__()
        open_t[name] = tt
        return h

    cols = {n: sb("c_" + n, [128, CC], F32)
            for n in ("mk_a", "mk_b", "mv_a", "mv_b", "mr_a", "mr_b", "lnw", "lnb",
                      "decay", "first", "lnD", "carry", "cf", "bnd", "xtb")}
    raw = {n: sb("r_" + n, [128, CC], F32)
           for n in ("tmk", "tmv", "tmr", "tdecay", "tfirst")}
    mprev = sb("mprev", [128, 1], F32)
    selfhot = sb("selfhot", [128, 8], F32)
    csel = sb("csel", [128, 8], F32)
    clipx = sb("clipx", [128, 1], F32)
    clipd = sb("clipd", [128, 1], F32)
    ident = sb("ident", [128, 128], BF16)
    iota_f = sb("iota_f", [128, 128], F32)
    iota_sl = sb("iota_sl", [128, tslab], F32)
    scal = sb("scal", [128, 16], F32)
    amax1 = sb("amax1", [128, 1], F32)
    amax2 = sb("amax2", [128, 1], F32)
    swd = sb("swd", [128, CC], F32)
    scall = sb("scall", [128, 8, nslot_gu], F32)
    stats = sb("stats", [128, 2, ngrp, 2, 6], F32)
    mvar = sb("mvar", [128, 2, ngrp, 2], F32)
    rsqc = sb("rsqc", [128, 2, ngrp], F32)
    betac = sb("betac", [128, 2, ngrp], F32)
    fmask = sb("fmask", [128, 8, CC], F32)
    fall = sb("fall", [128, 8, CC], F32)
    xbnd_tm = sb("xbnd_tm", [1, C], F32)
    xbnd_b = sb("xbnd_b", [1, C], BF16)
    bstat = sb("bstat", [1, 2, 6], F32)
    epsc = sb("epsc", [128, 1], F32)
    ones_row = sb("ones_row", [1, 128], BF16)
    abf = sb("abf", [128, 1], BF16)
    amsb = sb("amsb", [1, 1], BF16)
    amsc = sb("amsc", [1, 1], F32)
    bvar = sb("bvar", [1, 2], F32)
    ident1 = sb("ident1", [1, 1], BF16)

    pp = [ps(f"pp{i}", [128, 512], F32) for i in range(5)]
    pt = [ps(f"pt{i}", [128, 1024], BF16) for i in range(2)]
    ptw = ps("ptw", [128, 1024], BF16)
    PPK = [f"pp{i}" for i in range(5)]
    PTK = [f"pt{i}" for i in range(2)]

    p.dma("sync", "vin",
          [(raw[n][:], vec_in[n].rearrange("(j q) -> q j", q=128))
           for n in ("tmk", "tmv", "tmr", "tdecay", "tfirst")] +
          [(cols["lnw"][:], vec_in["lnw"].rearrange("(j q) -> q j", q=128)),
           (cols["lnb"][:], vec_in["lnb"].rearrange("(j q) -> q j", q=128)),
           (mprev[:], mprev_in[:]), (selfhot[:], selfhot_in[:]),
           (csel[:], csel_in[:]), (clipx[:], clipx_in[:]), (clipd[:], clipd_in[:]),
           (xbnd_tm[:], xb_in[:])],
          writes=("tmk", "tmv", "tmr", "tdecay", "tfirst", "lnw", "lnb", "mprev",
                  "selfhot", "csel", "clipx", "clipd", "xbnd"), slow=True)
    p.step("scalar", lambda e: e.activation(cols["decay"][:], raw["tdecay"][:], AF.Sigmoid),
           reads=("tdecay",), writes=("decay",))
    p.step("scalar", lambda e: e.activation(cols["first"][:], raw["tfirst"][:], AF.Exp),
           reads=("tfirst",), writes=("first",))
    p.step("scalar", lambda e: e.activation(cols["lnD"][:], cols["decay"][:], AF.Ln),
           reads=("decay",), writes=("lnD",))
    for nm, rw in (("mk", "tmk"), ("mv", "tmv"), ("mr", "tmr")):
        p.step("vector", lambda e, nm=nm, rw=rw: e.tensor_tensor(
            cols[nm + "_a"][:], raw[rw][:], cols["lnw"][:], ALU.mult),
            reads=(rw, "lnw"), writes=(nm + "_a",))
        p.step("vector", lambda e, nm=nm: e.tensor_sub(
            cols[nm + "_b"][:], cols["lnw"][:], cols[nm + "_a"][:]),
            reads=(nm + "_a", "lnw"), writes=(nm + "_b",))
    p.step("vector", lambda e: e.reciprocal(cols["bnd"][:], cols["lnw"][:]),
           reads=("lnw",), writes=("bnd",))
    p.step("vector", lambda e: e.tensor_tensor(cols["bnd"][:], cols["bnd"][:],
                                               cols["lnb"][:], ALU.mult),
           reads=("bnd", "lnb"), writes=("bnd",))
    p.step("vector", lambda e: e.tensor_scalar(cols["bnd"][:], cols["bnd"][:], -1.0,
                                               None, ALU.mult),
           reads=("bnd",), writes=("bnd",))
    p.step("gpsimd", lambda e: e.iota(iota_f[:], [[1, 128]], channel_multiplier=-1,
                                      allow_small_or_imprecise_dtypes=True),
           writes=("iota_f",))
    p.step("gpsimd", lambda e: e.tensor_scalar(ident[:], iota_f[:], 0.0, None,
                                               ALU.is_equal),
           reads=("iota_f",), writes=("ident",))
    p.step("gpsimd", lambda e: e.memset(ident1[:], 1.0), writes=("ident1",))
    p.step("gpsimd", lambda e: e.memset(ones_row[:], 1.0), writes=("ones_row",))
    p.step("gpsimd", lambda e: e.iota(iota_sl[:], [[1, tslab]], base=1,
                                      channel_multiplier=0,
                                      allow_small_or_imprecise_dtypes=True),
           writes=("iota_sl",))
    p.step("gpsimd", lambda e: e.memset(epsc[:], LN_EPS), writes=("epsc",))

    p.step("vector", lambda e: [e.bn_stats(bstat[:, 0, :], xbnd_tm[:, :512]),
                                e.bn_stats(bstat[:, 1, :], xbnd_tm[:, 512:])][-1],
           reads=("xbnd",), writes=("bstat",))
    p.step("vector", lambda e: e.bn_aggr(bvar[:], bstat[:]),
           reads=("bstat",), writes=("bvar",))
    p.step("scalar", lambda e: e.activation(bvar[:, 1:2], bvar[:, 1:2], AF.Sqrt,
                                            bias=epsc[0:1, :]),
           reads=("bvar", "epsc"), writes=("bvar",))
    p.step("vector", lambda e: e.reciprocal(bvar[:, 1:2], bvar[:, 1:2]),
           reads=("bvar",), writes=("bvar",))
    p.step("vector", lambda e: e.tensor_scalar(bvar[:, 0:1], bvar[:, 0:1], bvar[:, 1:2],
                                               -1.0, ALU.mult, ALU.mult),
           reads=("bvar",), writes=("bvar",))
    p.step("scalar", lambda e: e.activation(xbnd_b[:], xbnd_tm[:], AF.Identity,
                                            bias=bvar[:, 0:1], scale=bvar[:, 1:2]),
           reads=("xbnd", "bvar"), writes=("xbndb",))
    p.step("tensor", lambda e: [e.transpose(pt[0][:, 2 * cc2:2 * cc2 + 1],
                                            xbnd_b[:, cc2 * 128:(cc2 + 1) * 128],
                                            ident1[:]) for cc2 in range(CC)][-1],
           reads=("xbndb", "ident1"), writes=("pt0",))
    p.step("vector", lambda e: e.tensor_sub(
        cols["xtb"][:], pt[0][:, :2 * CC].rearrange("q (c w) -> q c w", w=2)[:, :, 0],
        cols["bnd"][:]),
        reads=("pt0", "bnd"), writes=("xtb",))
    p.step("vector", lambda e: e.scalar_tensor_tensor(
        cols["xtb"][:], cols["xtb"][:], mprev[:], cols["bnd"][:], ALU.mult, ALU.add),
        reads=("xtb", "mprev", "bnd"), writes=("xtb",))
    p.step("vector", lambda e: e.memset(cols["carry"][:], 0.0), writes=("carry",))

    wto = sb("wt_o", [128, CC, C], BF16)
    x_sb = sb("x_sb", [128, ngrp, C], F32)
    xstd = sb("xstd", [128, 2, C], BF16)
    xt_sb = sb("xt_sb", [128, CC, tslab + 1], F32)
    mixt = sb("mixt", [128, CC, thalf], BF16)
    xk_sb = sb("xk_sb", [128, 2, CC, thalf], BF16)
    xv_sb = sb("xv_sb", [128, 2, CC, thalf], BF16)
    xq_sb = sb("xq_sb", [128, 2, CC, thalf], BF16)
    k_sb = sb("k_sb", [128, 2, CC, tslab], BF16)
    r_sb = sb("r_sb", [128, CC, tslab], BF16)
    rv_sb = sb("rv_sb", [128, CC, tslab], BF16)
    s_sb = sb("s_sb", [128, 2, tslab], F32)
    t1_sb = sb("t1_sb", [128, CC, tslab], BF16)
    wt = {n: sb("wt_" + n, [128, CC, C], BF16) for n in ("k", "v", "r")}
    wld = sb("wld", [128, 2, 1024], F32)
    wcast = sb("wcast", [128, 2, 1024], BF16)
    wstage = sb("wstage", [128, 2, 1024], BF16)
    wq_v = sb("wq_v", [128, 1024], F32)
    wq_i = sb("wq_i", [128, 1024], mybir.dt.int32)
    wamax = sb("wamax", [128, 1], F32)
    wamt = sb("wamt", [128, 1], F32)
    sdcol = sb("sdcol", [128, 1], F32)
    P1_NAMES = ("x_sb", "xstd", "xt_sb", "mixt", "xk_sb", "xv_sb", "xq_sb", "k_sb",
                "r_sb", "rv_sb", "s_sb", "t1_sb", "wt_k", "wt_v", "wt_r", "wld",
                "wcast", "wstage", "wq_v", "wq_i", "wamax", "wamt", "sdcol",
                "scol_gu", "swd_inv")

    def WLT(par):
        return tuple(f"wld{par}{q}" for q in "abcd")

    def wld_load(eng, ch, par, src, w):
        h = w // 4
        for qi, q in enumerate("abcd"):
            p.dma(eng, f"{ch}{par}{q}", [(wld[:, par, qi * h:(qi + 1) * h],
                                          src[:, qi * h:(qi + 1) * h])],
                  reads=(f"wld{par}{q}",), writes=(f"wld{par}{q}",))

    def prep_attn_chunk(nm, win, wdst, wkey, dc, par):
        wld_load("sync", "wl", par, win[dc * 128:(dc + 1) * 128, :], C)
        p.step("vector", lambda e, par=par: e.tensor_copy(wcast[:, par, :],
                                                          wld[:, par, :]),
               reads=WLT(par), writes=(f"wcast{par}",))
        pk = PTK[par]
        p.step("tensor", lambda e, par=par: [
            e.transpose(pt[par][:, c2 * 128:(c2 + 1) * 128],
                        wcast[:, par, c2 * 128:(c2 + 1) * 128], ident[:])
            for c2 in range(CC)][-1],
            reads=(f"wcast{par}", "ident"), writes=(pk,))
        p.step("scalar", lambda e, dc=dc, par=par, wdst=wdst: e.activation(
            wdst[:, :, dc * 128:(dc + 1) * 128],
            pt[par][:].rearrange("q (c d) -> q c d", c=CC), AF.Identity),
            reads=(pk,), writes=(f"{wkey}{dc}",))

    scol_gu = sb("scol_gu", [128, nslot_gu], F32)
    swd_inv = sb("swd_inv", [128, CC], F32)

    wq_units = []

    def emit_down_scales():
        def unit():
            for sl3 in range(3):
                off = sl3 * 1024
                w = min(1024, icdim - off)
                if w <= 0:
                    continue
                wld_load("sync", "qdsc", sl3 % 2, wdsc_in[:, off:off + w], w)
                if sl3 == 0:
                    p.step("vector", lambda e, w=w, sl3=sl3: e.tensor_reduce(
                        wamax[:], wld[:, sl3 % 2, :w], AXX, ALU.max,
                        apply_absolute_value=True),
                        reads=WLT(sl3 % 2) + ("wamax", "wamt"), writes=("wamax",))
                else:
                    p.step("vector", lambda e, w=w, sl3=sl3: e.tensor_reduce(
                        wamt[:], wld[:, sl3 % 2, :w], AXX, ALU.max,
                        apply_absolute_value=True),
                        reads=WLT(sl3 % 2) + ("wamt",), writes=("wamt",))
                    p.step("vector", lambda e: e.tensor_max(wamax[:], wamax[:], wamt[:]),
                           reads=("wamt", "wamax"), writes=("wamax",))
            p.step("vector", lambda e: e.tensor_scalar(
                sdcol[:], wamax[:], 1.0 / QMAX, None, ALU.mult),
                reads=("wamax",), writes=("sdcol",))
            p.dma("gpsimd", "qsc", [(sc_sd_mine[0, :], sdcol[:])],
                  reads=("sdcol",), writes=("sdm",), slow=True)
            p.step("gpsimd", lambda e: e.collective_compute(
                "AllGather", ALU.bypass, RG, [sc_sd_mine[:]], [sc_sd_all[:]]),
                reads=("sdm",), writes=("ag_sd",))
            p.dma("scalar", "qdsr", [(swd[:, rc:rc + 1], sc_sd_all[rc, :])
                                     for rc in range(CC)],
                  reads=("ag_sd", "wsc_d"), writes=("wsc_d",), slow=True)
            p.step("vector", lambda e: e.reciprocal(swd_inv[:], swd[:]),
                   reads=("wsc_d", "wsd_inv"), writes=("wsd_inv",))
        unit()

    def emit_wo_units():
        def load(dc, par):
            wld_load("sync", "wl", par, w_in["Wo"][dc * 128:(dc + 1) * 128, :], C)

        def unit(dc, par):
            p.step("vector", lambda e, par=par: e.tensor_copy(wcast[:, par, :],
                                                              wld[:, par, :]),
                   reads=WLT(par), writes=(f"wcast{par}",))
            pk = PTK[par]
            p.step("tensor", lambda e, par=par: [
                e.transpose(pt[par][:, c2 * 128:(c2 + 1) * 128],
                            wcast[:, par, c2 * 128:(c2 + 1) * 128], ident[:])
                for c2 in range(CC)][-1],
                reads=(f"wcast{par}", "ident"), writes=(pk,))
            p.step("scalar", lambda e, dc=dc, par=par: e.activation(
                wto[:, :, dc * 128:(dc + 1) * 128],
                pt[par][:].rearrange("q (c d) -> q c d", c=CC), AF.Identity),
                reads=(pk,), writes=("wto",))
        for dc in range(CC):
            pu = len(wq_units) % 2
            wq_units.append((lambda dc=dc, pu=pu: load(dc, pu),
                             lambda dc=dc, pu=pu: unit(dc, pu)))

    def emit_gu_mine():
        def load(j, par):
            wld_load("sync", "ql", par, wgu_in[j * 128:(j + 1) * 128, :], C)

        def unit(j, par):
            p.step("vector", lambda e, par=par: e.tensor_reduce(
                wamax[:], wld[:, par, :C], AXX, ALU.max, apply_absolute_value=True),
                reads=WLT(par) + ("wamax", "wamt"), writes=("wamax",))
            p.step("vector", lambda e, j=j: e.tensor_scalar(
                scol_gu[:, j:j + 1], wamax[:], 1.0 / QMAX, None, ALU.mult),
                reads=("wamax",), writes=(f"scg{j}",))
            p.step("vector", lambda e: e.reciprocal(wamt[:], wamax[:]),
                   reads=("wamax",), writes=("wamt",))
            p.step("vector", lambda e, par=par: e.tensor_scalar(
                wq_v[:, :C], wld[:, par, :C], wamt[:], QMAX, ALU.mult, ALU.mult),
                reads=WLT(par) + ("wamt",), writes=("wq_v",))
            p.step("vector", lambda e: e.tensor_copy(wq_i[:, :C], wq_v[:, :C]),
                   reads=("wq_v", "wq_i"), writes=("wq_i",))
            p.step("vector", lambda e, par=par: e.tensor_copy(wcast[:, par, :C],
                                                              wq_i[:, :C]),
                   reads=("wq_i", f"wcast{par}"), writes=(f"wcast{par}",))
            p.step("tensor", lambda e, par=par: [
                e.transpose(ptw[:, c2 * 128:(c2 + 1) * 128],
                            wcast[:, par, c2 * 128:(c2 + 1) * 128], ident[:])
                for c2 in range(CC)][-1],
                reads=(f"wcast{par}", "ident", "ptw"), writes=("ptw",))
            p.step("scalar", lambda e, par=par: e.activation(
                wstage[:, par, :C], ptw[:], AF.Identity),
                reads=("ptw", f"wstage{par}"), writes=(f"wstage{par}",))
            p.dma("scalar", f"qs{par}", [(sc_gu_mine[j, :, :], wstage[:, par, :C])],
                  reads=(f"wstage{par}",), writes=(f"scm{j}",))
        for j in range(nslot_gu):
            pu = len(wq_units) % 2
            wq_units.append((lambda j=j, pu=pu: load(j, pu),
                             lambda j=j, pu=pu: unit(j, pu)))

    def emit_d_mine():
        def load(rc, par):
            wld_load("sync", "qd", par,
                     wdm_in[rc * 128:(rc + 1) * 128, :], nslot_d * 128)

        def unit(rc, par):
            for j in range(nslot_d):
                p.step("vector", lambda e, par=par, rc=rc, j=j: e.tensor_scalar(
                    wq_v[:, j * 128:(j + 1) * 128],
                    wld[:, par, j * 128:(j + 1) * 128],
                    swd_inv[:, rc:rc + 1], None, ALU.mult),
                    reads=WLT(par) + ("wsd_inv", "wq_v"), writes=("wq_v",))
            p.step("vector", lambda e: e.tensor_copy(wq_i[:, :nslot_d * 128],
                                                     wq_v[:, :nslot_d * 128]),
                   reads=("wq_v", "wq_i"), writes=("wq_i",))
            p.step("vector", lambda e, par=par: e.tensor_copy(
                wcast[:, par, :nslot_d * 128], wq_i[:, :nslot_d * 128]),
                reads=("wq_i", f"wcast{par}"), writes=(f"wcast{par}",))
            p.step("tensor", lambda e, par=par: [
                e.transpose(ptw[:, j2 * 128:(j2 + 1) * 128],
                            wcast[:, par, j2 * 128:(j2 + 1) * 128], ident[:])
                for j2 in range(nslot_d)][-1],
                reads=(f"wcast{par}", "ident", "ptw"), writes=("ptw",))
            p.step("scalar", lambda e, par=par: e.activation(
                wstage[:, par, :nslot_d * 128], ptw[:, :nslot_d * 128], AF.Identity),
                reads=("ptw", f"wstage{par}"), writes=(f"wstage{par}",))
            p.dma("scalar", f"qs{par}",
                  [(sc_d_mine[j, :, rc * 128:(rc + 1) * 128],
                    wstage[:, par, j * 128:(j + 1) * 128]) for j in range(nslot_d)],
                  reads=(f"wstage{par}",), writes=(f"scdm{rc}",))
        for rc in range(CC):
            pu = len(wq_units) % 2
            wq_units.append((lambda rc=rc, pu=pu: load(rc, pu),
                             lambda rc=rc, pu=pu: unit(rc, pu)))

    def emit_wq_allgathers():
        p.step("gpsimd", lambda e: e.collective_compute(
            "AllGather", ALU.bypass, RG, [sc_gu_mine[:]], [sc_gu_all[:]]),
            reads=tuple(f"scm{j}" for j in range(nslot_gu)), writes=("ag_gu",))
        p.step("gpsimd", lambda e: e.collective_compute(
            "AllGather", ALU.bypass, RG, [sc_d_mine[:]], [sc_d_all[:]]),
            reads=tuple(f"scdm{rc}" for rc in range(CC)), writes=("ag_d",))

    emit_wo_units()
    emit_gu_mine()
    emit_d_mine()

    wq_ci = [0]
    wq_li = [0]

    def drain_wq(n):
        for _ in range(min(n, len(wq_units) - wq_ci[0])):
            while wq_li[0] < len(wq_units) and wq_li[0] < wq_ci[0] + 2:
                wq_units[wq_li[0]][0]()
                wq_li[0] += 1
            wq_units[wq_ci[0]][1]()
            wq_ci[0] += 1
            if wq_li[0] < len(wq_units):
                wq_units[wq_li[0]][0]()
                wq_li[0] += 1

    emit_down_scales()

    def p1_xload(s):
        p.dma("sync", "xin", [(x_sb[:], x_in[s * tslab:(s + 1) * tslab, :]
                               .rearrange("(g q) c -> q g c", q=128))],
              reads=("xs",), writes=("xs",))

    def p1_head_ln(s):
        par = s % 2
        for g in range(ngrp):
            p.step("vector", lambda e, g=g, par=par: [
                e.bn_stats(stats[:, par, g, 0, :], x_sb[:, g, 0:512]),
                e.bn_stats(stats[:, par, g, 1, :], x_sb[:, g, 512:1024])][-1],
                reads=("xs", f"st{par}_{g}"), writes=(f"st{par}_{g}",))
            p.step("vector", lambda e, g=g, par=par: e.bn_aggr(
                mvar[:, par, g, :], stats[:, par, g, :, :]),
                reads=(f"st{par}_{g}", f"mv{par}"), writes=(f"mv{par}",))
        p.step("scalar", lambda e, par=par: e.activation(
            rsqc[:, par], mvar[:, par, :, 1], AF.Sqrt, bias=epsc[:]),
            reads=(f"mv{par}", "epsc", f"rsq{par}"), writes=(f"rsq{par}",))
        p.step("vector", lambda e, par=par: e.reciprocal(rsqc[:, par], rsqc[:, par]),
               reads=(f"rsq{par}",), writes=(f"rsq{par}",))
        p.step("vector", lambda e, par=par: e.tensor_tensor(
            betac[:, par], mvar[:, par, :, 0], rsqc[:, par], ALU.mult),
            reads=(f"rsq{par}", f"mv{par}", f"bet{par}"), writes=(f"bet{par}",))
        p.step("vector", lambda e, par=par: e.tensor_scalar(
            betac[:, par], betac[:, par], -1.0, None, ALU.mult),
            reads=(f"bet{par}",), writes=(f"bet{par}",))
        p.step("vector", lambda e: e.tensor_copy(xt_sb[:, :, 0], cols["xtb"][:]),
               reads=("xtb", "xt_c0"), writes=("xt_c0",))

    def p1_head_xt(s):
        par = s % 2
        for g in range(ngrp):
            p.step("scalar", lambda e, g=g, par=par: e.activation(
                xstd[:, g % 2, :], x_sb[:, g, :], AF.Identity,
                bias=betac[:, par, g:g + 1], scale=rsqc[:, par, g:g + 1]),
                reads=("xs", f"rsq{par}", f"bet{par}", f"xstd{g % 2}"),
                writes=(f"xstd{g % 2}",))
            pk = PTK[g % 2]
            p.step("tensor", lambda e, g=g: [
                e.transpose(pt[g % 2][:, c2 * 128:(c2 + 1) * 128],
                            xstd[:, g % 2, c2 * 128:(c2 + 1) * 128], ident[:])
                for c2 in range(CC)][-1],
                reads=(f"xstd{g % 2}", "ident"), writes=(pk,))
            p.step("scalar", lambda e, g=g: e.activation(
                xt_sb[:, :, g * 128 + 1:g * 128 + 129],
                pt[g % 2][:].rearrange("q (c t) -> q c t", c=CC), AF.Identity),
                reads=(pk, f"xt{g}"), writes=(f"xt{g}",))
        xt_keys = tuple(f"xt{g}" for g in range(ngrp)) + ("xt_c0",)
        p.step("vector", lambda e: e.tensor_copy(cols["xtb"][:], xt_sb[:, :, tslab]),
               reads=xt_keys + ("xtb",), writes=("xtb",))

    def p1_head_mix(s, h):
        par = s % 2
        if h == 0:
            xt_keys = ("xt_c0", "xt0", "xt1")
        else:
            xt_keys = ("xt1", "xt2", "xt3")
        if True:
            off = h * thalf
            for nm, buf, bk in (("mk", xk_sb, "xk"), ("mv", xv_sb, "xv"),
                                ("mr", xq_sb, "xq")):
                p.step("scalar", lambda e, nm=nm, off=off: [
                    e.activation(mixt[:, c2, :], xt_sb[:, c2, off + 1:off + thalf + 1],
                                 AF.Identity, bias=cols["lnb"][:, c2:c2 + 1],
                                 scale=cols[nm + "_a"][:, c2:c2 + 1])
                    for c2 in range(CC)][-1],
                    reads=xt_keys + (nm + "_a", "lnb", "mixt"), writes=("mixt",))
                p.step("vector", lambda e, nm=nm, buf=buf, h=h, off=off: [
                    e.scalar_tensor_tensor(buf[:, h, c2, :],
                                           xt_sb[:, c2, off:off + thalf],
                                           cols[nm + "_b"][:, c2:c2 + 1],
                                           mixt[:, c2, :], ALU.mult, ALU.add)
                    for c2 in range(CC)][-1],
                    reads=xt_keys + (nm + "_b", "mixt", f"{bk}{h}"),
                    writes=(f"{bk}{h}",))

    def p1_mid_dc(s, h, dc):
        par = s % 2
        if True:
            off = h * thalf
            if True:
                pr, kr = (pp[0], "pp0") if dc % 2 == 0 else (pp[3], "pp3")
                p.step("tensor", lambda e, dc=dc, h=h, pr=pr: [
                    e.matmul(pr[:, :thalf], wt["r"][:, c2, dc * 128:(dc + 1) * 128],
                             xq_sb[:, h, c2, :], start=(c2 == 0), stop=(c2 == CC - 1))
                    for c2 in range(CC)][-1],
                    reads=(f"xq{h}", f"wtr{dc}"), writes=(kr,))
                p.step("scalar", lambda e, dc=dc, pr=pr, off=off: e.activation(
                    r_sb[:, dc, off:off + thalf], pr[:, :thalf], AF.Sigmoid),
                    reads=(kr, f"r{dc}"), writes=(f"r{dc}",))
                pv, kv = (pp[1], "pp1") if dc % 2 == 0 else (pp[4], "pp4")
                p.step("tensor", lambda e, dc=dc, h=h, pv=pv: [
                    e.matmul(pv[:, :thalf], wt["v"][:, c2, dc * 128:(dc + 1) * 128],
                             xv_sb[:, h, c2, :], start=(c2 == 0), stop=(c2 == CC - 1))
                    for c2 in range(CC)][-1],
                    reads=(f"xv{h}", f"wtv{dc}"), writes=(kv,))
                p.step("scalar", lambda e, dc=dc, off=off, pv=pv: e.activation(
                    rv_sb[:, dc, off:off + thalf], pv[:, :thalf], AF.Identity),
                    reads=(kv, f"rv_{dc}"), writes=(f"rv_{dc}",))
                p.step("tensor", lambda e, dc=dc, h=h: [
                    e.matmul(pp[2][:, :thalf], wt["k"][:, c2, dc * 128:(dc + 1) * 128],
                             xk_sb[:, h, c2, :], start=(c2 == 0), stop=(c2 == CC - 1))
                    for c2 in range(CC)][-1],
                    reads=(f"xk{h}", f"wtk{dc}"), writes=("pp2",))
                p.step("scalar", lambda e, dc=dc, par=par, off=off: e.activation(
                    k_sb[:, par, dc, off:off + thalf], pp[2][:, :thalf], AF.Identity),
                    reads=("pp2", f"k{par}_{dc}"), writes=(f"k{par}_{dc}",))

    def p1_mid(s, h):
        for dc in range(CC):
            p1_mid_dc(s, h, dc)

    def p1_tail(s):
        par = s % 2
        for c2 in range(CC):
            p2c = c2 % 2
            p.step("vector", lambda e, c2=c2, par=par, p2c=p2c: e.tensor_tensor_scan(
                s_sb[:, p2c, :], cols["decay"][:, c2:c2 + 1].broadcast_to([128, tslab]),
                k_sb[:, par, c2, :], cols["carry"][:, c2:c2 + 1], ALU.mult, ALU.add),
                reads=(f"k{par}_{c2}", "decay", "carry", f"s{p2c}"), writes=(f"s{p2c}",))
            p.step("vector", lambda e, c2=c2, p2c=p2c: e.tensor_copy(
                cols["carry"][:, c2:c2 + 1], s_sb[:, p2c, tslab - 1:tslab]),
                reads=(f"s{p2c}", "carry"), writes=("carry",))
            p.step("vector", lambda e, c2=c2, par=par, p2c=p2c: e.scalar_tensor_tensor(
                t1_sb[:, c2, :], s_sb[:, p2c, :], cols["first"][:, c2:c2 + 1],
                k_sb[:, par, c2, :], ALU.mult, ALU.add),
                reads=(f"s{p2c}", "first", f"k{par}_{c2}", f"t1_{c2}"),
                writes=(f"t1_{c2}",))
        for scr, buf, rkeys, ch in (
                (sc_t1, t1_sb, tuple(f"t1_{c}" for c in range(CC)), "spA"),
                (sc_rv, rv_sb, tuple(f"rv_{c}" for c in range(CC)), "spB"),
                (sc_r2, r_sb, tuple(f"r{c}" for c in range(CC)), "spC")):
            p.dma("gpsimd", f"{ch}{par}",
                  [(scr[:, :, s * tslab:(s + 1) * tslab].rearrange("c q t -> q c t"),
                    buf[:])],
                  reads=rkeys, writes=(f"spill{s}{ch}",))

    p1_xload(0)
    p1_head_ln(0)
    p1_head_xt(0)
    p1_head_mix(0, 0)
    pprep = 0
    for dc in range(CC):
        for nm in ("r", "v", "k"):
            prep_attn_chunk(nm, w_in["W" + nm], wt[nm], "wt" + nm, dc, pprep % 2)
            pprep += 1
        p1_mid_dc(0, 0, dc)
    p1_head_mix(0, 1)
    if nslab > 1:
        p1_xload(1)
    for s in range(nslab):
        if s + 1 < nslab:
            p1_head_ln(s + 1)
        if s > 0:
            p1_mid(s, 0)
        if s + 1 < nslab:
            p1_head_xt(s + 1)
        drain_wq(8)
        if s + 1 < nslab:
            p1_head_mix(s + 1, 0)
        p1_mid(s, 1)
        p1_tail(s)
        if s + 1 < nslab:
            p1_head_mix(s + 1, 1)
        if s + 2 < nslab:
            p1_xload(s + 2)
    drain_wq(len(wq_units))

    scmv = wq_v[:, 0:8 * nslot_gu].rearrange("q (r j) -> q r j", j=nslot_gu)
    for j in range(nslot_gu):
        p.step("vector", lambda e, j=j: e.tensor_scalar(
            scmv[:, :, j], selfhot[:], scol_gu[:, j:j + 1], None, ALU.mult),
            reads=(f"scg{j}", "selfhot", "wq_v"), writes=("wq_v",))
    p.dma("sync", "scm", [(sc_scm_i[:], wq_v[:, 0:8 * nslot_gu])],
          reads=("wq_v",), writes=("scmi",))

    p.step("vector", lambda e: [e.tensor_scalar(fmask[:, r2, :], cols["carry"][:],
                                                selfhot[:, r2:r2 + 1], None, ALU.mult)
                                for r2 in range(8)][-1],
           reads=("carry", "selfhot"), writes=("fmask",))
    p.dma("sync", "cc1", [(cc_c_i[:], fmask[:].rearrange("q r c -> q (r c)"))],
          reads=("fmask",), writes=("ccci",))

    p.barrier()
    p.step("gpsimd", lambda e: e.collective_compute(
        "AllReduce", ALU.add, RG, [cc_c_i[:]], [cc_c_o[:]]),
        reads=("ccci",), writes=("ccco",))
    p.step("gpsimd", lambda e: e.collective_compute(
        "AllReduce", ALU.add, RG, [sc_scm_i[:]], [sc_scm_o[:]]),
        reads=("scmi",), writes=("scmo",))
    emit_wq_allgathers()
    p.dma("sync", "scr", [(scall[:].rearrange("q r j -> q (r j)"), sc_scm_o[:])],
          reads=("scmo", "scall"), writes=("scall",))
    sb_free(*P1_NAMES)
    h1 = sb("h1", [128, ngq, C], BF16)
    xr_sb = sb("xr_sb", [128, 2, ngrp, C], F32)
    t1r = sb("t1r", [128, 2, CC, tslab], BF16)
    rvr = sb("rvr", [128, 2, CC, tslab], BF16)
    rrr = sb("rrr", [128, 2, CC, tslab], BF16)
    dpow = sb("dpow", [128, CC, tslab], BF16)
    rwk = sb("rwk", [128, 2, CC, tslab], BF16)
    y_sb = sb("y_sb", [128, 2, CC, tslab], BF16)
    h1p = sb("h1p", [128, ngrp, C], F32)
    P2_NAMES = ("xr_sb", "t1r", "rvr", "rrr", "dpow", "rwk", "y_sb", "h1p")
    p.step("vector", lambda e: e.memset(amax1[:], 0.0), writes=("amax1",))

    def cc1_finish():
        p.dma("sync", "cc1", [(fall[:].rearrange("q r c -> q (r c)"), cc_c_o[:])],
              reads=("ccco",), writes=("fall",))
        p.step("vector", lambda e: e.tensor_scalar(cols["carry"][:], fall[:, 0, :],
                                                   csel[:, 0:1], None, ALU.mult),
               reads=("fall", "csel", "carry"), writes=("carry",))
        for r2 in range(1, 8):
            p.step("vector", lambda e, r2=r2: e.scalar_tensor_tensor(
                cols["carry"][:], fall[:, r2, :], csel[:, r2:r2 + 1],
                cols["carry"][:], ALU.mult, ALU.add),
                reads=("fall", "csel", "carry"), writes=("carry",))
        p.step("vector", lambda e: e.tensor_tensor(cols["cf"][:], cols["carry"][:],
                                                   cols["first"][:], ALU.mult),
               reads=("carry", "first"), writes=("cf",))

    def p2_restream(s):
        par = s % 2
        for buf, scr, tg, ch, sch in ((t1r, sc_t1, "t1r", "rsA", "spA"),
                                      (rvr, sc_rv, "rvr", "rsB", "spB"),
                                      (rrr, sc_r2, "rrr", "rsC", "spC")):
            p.dma("scalar", f"{ch}{par}",
                  [(buf[:, par], scr[:, :, s * tslab:(s + 1) * tslab]
                    .rearrange("c q t -> q c t"))],
                  reads=(f"spill{s}{sch}", f"{tg}{par}"), writes=(f"{tg}{par}",))

    def p2_comp(s):
        par = s % 2
        p.dma("sync", f"xr{par}", [(xr_sb[:, par], x_in[s * tslab:(s + 1) * tslab, :]
                                    .rearrange("(g q) c -> q g c", q=128))],
              reads=(f"xr{par}",), writes=(f"xr{par}",))
        if s > 0:
            p.step("vector", lambda e: e.tensor_scalar(
                iota_sl[:], iota_sl[:], float(tslab), None, ALU.add),
                reads=("iota_sl",), writes=("iota_sl",))
        p.step("scalar", lambda e: [
            e.activation(dpow[:, c2, :], iota_sl[:], AF.Exp,
                         scale=cols["lnD"][:, c2:c2 + 1]) for c2 in range(CC)][-1],
            reads=("iota_sl", "lnD", "dpow"), writes=("dpow",))
        p.step("vector", lambda e, par=par: [
            e.tensor_tensor(rvr[:, par, c2, :], rvr[:, par, c2, :],
                            rrr[:, par, c2, :], ALU.mult) for c2 in range(CC)][-1],
            reads=(f"rvr{par}", f"rrr{par}"), writes=(f"rvr{par}",))
        p.step("vector", lambda e, par=par: [
            e.scalar_tensor_tensor(rwk[:, par, c2, :], dpow[:, c2, :],
                                   cols["cf"][:, c2:c2 + 1], t1r[:, par, c2, :],
                                   ALU.mult, ALU.add) for c2 in range(CC)][-1],
            reads=("dpow", "cf", f"t1r{par}", f"rwk{par}"), writes=(f"rwk{par}",))
        p.step("vector", lambda e, par=par: [
            e.tensor_tensor(rwk[:, par, c2, :], rwk[:, par, c2, :],
                            rvr[:, par, c2, :], ALU.mult) for c2 in range(CC)][-1],
            reads=(f"rwk{par}", f"rvr{par}"), writes=(f"rwk{par}",))

    def p2_mid(s):
        par = s % 2
        for dc in range(CC):
            bi = dc % 4
            p.step("tensor", lambda e, dc=dc, bi=bi, par=par: [
                e.matmul(pp[bi][:, :tslab],
                         wto[:, c2, dc * 128:(dc + 1) * 128], rwk[:, par, c2, :],
                         start=(c2 == 0), stop=(c2 == CC - 1)) for c2 in range(CC)][-1],
                reads=(f"rwk{par}", "wto"), writes=(f"pp{bi}",))
            p.step("scalar", lambda e, dc=dc, bi=bi, par=par: e.activation(
                y_sb[:, par, dc, :], pp[bi][:, :tslab], AF.Identity),
                reads=(f"pp{bi}", f"y{par}_{dc}"), writes=(f"y{par}_{dc}",))

    def p2_tail(s):
        par = s % 2
        ykeys = tuple(f"y{par}_{d}" for d in range(CC))
        for g in range(ngrp):
            pk = PTK[g % 2]
            p.step("tensor", lambda e, g=g, par=par: [
                e.transpose(pt[g % 2][:, c2 * 128:(c2 + 1) * 128],
                            y_sb[:, par, c2, g * 128:(g + 1) * 128], ident[:])
                for c2 in range(CC)][-1],
                reads=ykeys + ("ident",), writes=(pk,))
            p.step("vector", lambda e, g=g, par=par: e.scalar_tensor_tensor(
                h1p[:, g, :], xr_sb[:, par, g, :], 2.0, pt[g % 2][:],
                ALU.mult, ALU.add),
                reads=(pk, f"xr{par}", f"h1p{g}"), writes=(f"h1p{g}",))
            p.step("vector", lambda e, g=g, par=par: [
                e.bn_stats(stats[:, par, g, 0, :], h1p[:, g, 0:512]),
                e.bn_stats(stats[:, par, g, 1, :], h1p[:, g, 512:1024])][-1],
                reads=(f"h1p{g}", f"st{par}_{g}"), writes=(f"st{par}_{g}",))
            p.step("vector", lambda e, g=g, par=par: e.bn_aggr(
                mvar[:, par, g, :], stats[:, par, g, :, :]),
                reads=(f"st{par}_{g}", f"mv{par}"), writes=(f"mv{par}",))
        p.step("scalar", lambda e, par=par: e.activation(
            rsqc[:, par], mvar[:, par, :, 1], AF.Sqrt, bias=epsc[:]),
            reads=(f"mv{par}", "epsc", f"rsq{par}"), writes=(f"rsq{par}",))
        p.step("vector", lambda e, par=par: e.reciprocal(rsqc[:, par], rsqc[:, par]),
               reads=(f"rsq{par}",), writes=(f"rsq{par}",))
        p.step("vector", lambda e, par=par: e.tensor_tensor(
            betac[:, par], mvar[:, par, :, 0], rsqc[:, par], ALU.mult),
            reads=(f"rsq{par}", f"mv{par}", f"bet{par}"), writes=(f"bet{par}",))
        p.step("vector", lambda e, par=par: e.tensor_scalar(
            betac[:, par], betac[:, par], -1.0, None, ALU.mult),
            reads=(f"bet{par}",), writes=(f"bet{par}",))
        for g in range(ngrp):
            gg = s * ngrp + g
            p.step("scalar", lambda e, g=g, gg=gg, par=par: e.activation(
                h1[:, gg, :], h1p[:, g, :], AF.Identity,
                bias=betac[:, par, g:g + 1], scale=rsqc[:, par, g:g + 1]),
                reads=(f"h1p{g}", f"rsq{par}", f"bet{par}"), writes=(f"h1_{gg}",))
            p.step("vector", lambda e, gg=gg: e.tensor_reduce(
                scal[:, 8:9], h1[:, gg, :], AXX, ALU.max, apply_absolute_value=True),
                reads=(f"h1_{gg}", "amt"), writes=("amt",))
            p.step("vector", lambda e: e.tensor_max(amax1[:], amax1[:], scal[:, 8:9]),
                   reads=("amt", "amax1"), writes=("amax1",))

    p2_restream(0)
    if nslab > 1:
        p2_restream(1)
    cc1_finish()
    p2_comp(0)
    for s in range(nslab):
        p2_mid(s)
        if s + 2 < nslab:
            p2_restream(s + 2)
        if s + 1 < nslab:
            p2_comp(s + 1)
        p2_tail(s)

    def cc_absmax(amax, cin, cout, key, ch, hr):
        p.step("vector", lambda e: e.tensor_copy(abf[:], amax[:]),
               reads=(key, "abf"), writes=("abf",))
        p.step("tensor", lambda e: e.transpose(pt[0][0:1, 0:128], abf[:], ident[:]),
               reads=("abf", "ident", PTK[0]), writes=(PTK[0],))
        p.step("vector", lambda e: e.tensor_reduce(amsc[:], pt[0][0:1, 0:128],
                                                   AXX, ALU.max),
               reads=(PTK[0], "amsc"), writes=("amsc",))
        p.dma("sync", ch, [(cin[:], amsc[:])], reads=("amsc",), writes=(key + "i",))
        p.step("gpsimd", lambda e: e.collective_compute(
            "AllReduce", ALU.max, RG, [cin[:]], [cout[:]]),
            reads=(key + "i",), writes=(key + "o",))
        p.dma("sync", ch, [(amsc[:], cout[:])], reads=(key + "o", "amsc"),
              writes=("amsc",))
        p.step("vector", lambda e: e.tensor_copy(amsb[:], amsc[:]),
               reads=("amsc", "amsb"), writes=("amsb",))
        p.step("tensor", lambda e: e.matmul(pp[3][:, 0:1], ones_row[:], amsb[:],
                                            start=True, stop=True),
               reads=("amsb", "ones_row", "pp3"), writes=("pp3",))
        p.step("vector", lambda e: e.tensor_copy(amax[:], pp[3][:, 0:1]),
               reads=("pp3", key), writes=(key,))

    cc_absmax(amax1, cc_m_i[0], cc_m_o[0], "amax1", "ccm1", hrow[0])
    p.step("scalar", lambda e: e.activation(scal[:, 6:7], clipx[:], AF.Sigmoid),
           reads=("clipx",), writes=("sgx",))
    p.step("vector", lambda e: e.tensor_scalar(scal[:, 1:2], amax1[:], scal[:, 6:7],
                                               1.0 / QMAX, ALU.mult, ALU.mult),
           reads=("amax1", "sgx"), writes=("scale1",))
    p.step("vector", lambda e: e.reciprocal(scal[:, 2:3], scal[:, 1:2]),
           reads=("scale1",), writes=("inv1",))

    p.barrier()
    sb_free(*P2_NAMES)
    xq1_fm = sb("xq1_fm", [128, CC, tcore], BF16)
    q_v = sb("q_v", [128, 2, C], F32)
    q_i = sb("q_i", [128, C], mybir.dt.int32)
    xq1_tm = sb("xq1_tm", [128, C], BF16)
    wgT = sb("wgT", [128, 2, CC, 128], BF16)
    wuT = sb("wuT", [128, 2, CC, 128], BF16)
    sg_sb = sb("sg_sb", [128, 2, tcore], BF16)
    h2_sb = sb("h2_sb", [128, 2, tcore], F32)
    sccol = sb("sccol", [128, 4], F32)
    P3_NAMES = ("xq1_fm", "q_v", "q_i", "xq1_tm", "wgT", "wuT", "sg_sb", "h2_sb",
                "sccol")

    for gg in range(ngq):
        par = gg % 2
        p.step("vector", lambda e, gg=gg, par=par: e.tensor_scalar(
            q_v[:, par, :], h1[:, gg, :], scal[:, 2:3], QMAX, ALU.mult, ALU.min),
            reads=(f"h1_{gg}", "inv1", f"qv{par}"), writes=(f"qv{par}",))
        p.step("vector", lambda e, par=par: e.tensor_scalar(
            q_i[:], q_v[:, par, :], -QMAX - 1.0, None, ALU.max),
            reads=(f"qv{par}", "qi"), writes=("qi",))
        p.step("scalar", lambda e: e.activation(xq1_tm[:], q_i[:], AF.Identity),
               reads=("qi", "xq1tm"), writes=("xq1tm",))
        pk = PTK[par]
        p.step("tensor", lambda e, par=par: [
            e.transpose(pt[par][:, c2 * 128:(c2 + 1) * 128],
                        xq1_tm[:, c2 * 128:(c2 + 1) * 128], ident[:])
            for c2 in range(CC)][-1],
            reads=("xq1tm", "ident"), writes=(pk,))
        p.step("scalar", lambda e, gg=gg, par=par: e.activation(
            xq1_fm[:, :, gg * 128:(gg + 1) * 128],
            pt[par][:].rearrange("q (c t) -> q c t", c=CC), AF.Identity),
            reads=(pk,), writes=(f"xq1f{gg}",))
    xq1_keys = tuple(f"xq1f{gg}" for gg in range(ngq))

    p.step("vector", lambda e: e.memset(amax2[:], 0.0), writes=("amax2",))
    ic_n_ = ic_n
    ntB = max(1, tcore // 512)
    tB = min(512, tcore)
    for ic in range(ic_n_):
        par = ic % 2
        gg_, gu_ = ic, ic_n + ic
        p.dma("sync", f"wb{par}",
              [(wgT[:, par].rearrange("q c d -> q (c d)"),
                sc_gu_all[gg_ % 8, gg_ // 8, :, :]),
               (wuT[:, par].rearrange("q c d -> q (c d)"),
                sc_gu_all[gu_ % 8, gu_ // 8, :, :])],
              reads=("ag_gu", f"wgT{par}", f"wuT{par}"),
              writes=(f"wgT{par}", f"wuT{par}"))
        p.step("vector", lambda e, par=par, gg_=gg_, gu_=gu_: [
            e.tensor_tensor(sccol[:, par * 2:par * 2 + 1],
                            scall[:, gg_ % 8, gg_ // 8:gg_ // 8 + 1],
                            scal[:, 1:2], ALU.mult),
            e.tensor_tensor(sccol[:, par * 2 + 1:par * 2 + 2],
                            scall[:, gu_ % 8, gu_ // 8:gu_ // 8 + 1],
                            scal[:, 1:2], ALU.mult)][-1],
            reads=("scall", "scale1", f"scc{par}"), writes=(f"scc{par}",))
        for tti in range(ntB):
            bg, bu = 2 * (tti % 2), 2 * (tti % 2) + 1
            p.step("tensor", lambda e, par=par, tti=tti, bg=bg: [
                e.matmul(pp[bg][:, :tB], wgT[:, par, c2, :],
                         xq1_fm[:, c2, tti * tB:(tti + 1) * tB],
                         start=(c2 == 0), stop=(c2 == CC - 1)) for c2 in range(CC)][-1],
                reads=(f"wgT{par}",) + xq1_keys, writes=(f"pp{bg}",))
            p.step("scalar", lambda e, par=par, tti=tti, bg=bg: e.activation(
                sg_sb[:, par, tti * tB:(tti + 1) * tB], pp[bg][:, :tB], AF.Silu,
                scale=sccol[:, par * 2:par * 2 + 1]),
                reads=(f"pp{bg}", f"scc{par}", f"sg{par}"), writes=(f"sg{par}",))
            p.step("tensor", lambda e, par=par, tti=tti, bu=bu: [
                e.matmul(pp[bu][:, :tB], wuT[:, par, c2, :],
                         xq1_fm[:, c2, tti * tB:(tti + 1) * tB],
                         start=(c2 == 0), stop=(c2 == CC - 1)) for c2 in range(CC)][-1],
                reads=(f"wuT{par}",) + xq1_keys, writes=(f"pp{bu}",))
            p.step("vector", lambda e, par=par, tti=tti, bu=bu: e.scalar_tensor_tensor(
                h2_sb[:, par, tti * tB:(tti + 1) * tB], pp[bu][:, :tB],
                sccol[:, par * 2 + 1:par * 2 + 2], sg_sb[:, par, tti * tB:(tti + 1) * tB],
                ALU.mult, ALU.mult),
                reads=(f"pp{bu}", f"sg{par}", f"scc{par}", f"h2{par}"),
                writes=(f"h2{par}",))
            p.step("vector", lambda e, par=par, tti=tti: e.tensor_reduce(
                scal[:, 9:10], h2_sb[:, par, tti * tB:(tti + 1) * tB], AXX, ALU.max,
                apply_absolute_value=True),
                reads=(f"h2{par}", "am2t"), writes=("am2t",))
            p.step("vector", lambda e: e.tensor_max(amax2[:], amax2[:], scal[:, 9:10]),
                   reads=("am2t", "amax2"), writes=("amax2",))
        p.dma("gpsimd", f"h2o{par}", [(sc_h2[ic, :, :], h2_sb[:, par, :])],
              reads=(f"h2{par}",), writes=(f"sch2_{ic}",))

    cc_absmax(amax2, cc_m_i[1], cc_m_o[1], "amax2", "ccm2", hrow[1])
    p.step("scalar", lambda e: e.activation(scal[:, 7:8], clipd[:], AF.Sigmoid),
           reads=("clipd",), writes=("sgd",))
    p.step("vector", lambda e: e.tensor_scalar(scal[:, 4:5], amax2[:], scal[:, 7:8],
                                               1.0 / QMAX, ALU.mult, ALU.mult),
           reads=("amax2", "sgd"), writes=("scale2",))
    p.step("vector", lambda e: e.reciprocal(scal[:, 5:6], scal[:, 4:5]),
           reads=("scale2",), writes=("inv2",))
    p.step("vector", lambda e: e.tensor_scalar(swd[:], swd[:], scal[:, 4:5], None,
                                               ALU.mult),
           reads=("scale2", "wsc_d"), writes=("swd2",))

    p.barrier()
    sb_free(*P3_NAMES)
    wdT = sb("wdT", [128, ic_n, C], BF16)
    v2i = sb("v2i", [128, 2, ic_n, thalf], BF16)
    v2f = sb("v2f", [128, (ic_n + 1) // 2, thalf], F32)
    f2i = sb("f2i", [128, (ic_n + 1) // 2, thalf], mybir.dt.int32)
    xq2 = sb("xq2", [128, 2, ic_n, thalf], BF16)
    dn_sb = sb("dn_sb", [128, 2, CC, thalf], BF16)
    o_pre = sb("o_pre", [128, thalf // 128, C], F32)
    o_sb = sb("o_sb", [128, 2, thalf // 128, C], F32)
    ostats = sb("ostats", [128, thalf // 128, 2, 6], F32)
    omv = sb("omv", [128, thalf // 128, 2], F32)
    orsq = sb("orsq", [128, thalf // 128], F32)
    obeta = sb("obeta", [128, thalf // 128], F32)
    P4_NAMES = ("wdT", "v2i", "v2f", "f2i", "xq2", "dn_sb", "o_pre", "o_sb", "ostats", "omv", "orsq", "obeta")

    if dbg:
        p.dma("sync", "dbgd",
              [(dbg_h1[:].rearrange("(g q) c -> q g c", q=128), h1[:])] +
              [(dbg_m[:, 0:1], amax1[0:1, :]), (dbg_m[:, 1:2], amax2[0:1, :]),
               (dbg_m[:, 2:3], scal[0:1, 2:3]), (dbg_m[:, 3:4], scal[0:1, 5:6])],
              reads=tuple(f"h1_{gg}" for gg in range(ngq)) + ("amax1", "amax2", "inv1", "inv2"),
              writes=("dbgout",))
    half_ic = (ic_n + 1) // 2
    p.dma("sync", "wdl",
          [(wdT[:, ic2, :], sc_d_all[ic2 % 8, ic2 // 8, :, :])
           for ic2 in range(ic_n)],
          reads=("ag_d",), writes=("wdT",))
    nch = tcore // thalf
    og = thalf // 128

    def p4_head(hs):
        par = hs % 2
        p.dma("gpsimd", f"h2i{par}",
              [(v2i[:, par, :half_ic, :],
                sc_h2[:half_ic, :, hs * thalf:(hs + 1) * thalf]
                .rearrange("i q t -> q i t")),
               (v2i[:, par, half_ic:, :],
                sc_h2[half_ic:, :, hs * thalf:(hs + 1) * thalf]
                .rearrange("i q t -> q i t"))],
              reads=tuple(f"sch2_{ic}" for ic in range(ic_n)) + (f"v2i{par}",),
              writes=(f"v2i{par}",))
        for sub in range(2):
            i0 = sub * half_ic
            i1 = min(ic_n, i0 + half_ic)
            ni = i1 - i0
            if ni <= 0:
                continue
            p.step("vector", lambda e, par=par, i0=i0, ni=ni: e.tensor_scalar(
                v2f[:, :ni, :].rearrange("q i t -> q (i t)"),
                v2i[:, par, i0:i0 + ni, :].rearrange("q i t -> q (i t)"),
                scal[:, 5:6], QMAX, ALU.mult, ALU.min),
                reads=(f"v2i{par}", "inv2", "v2f"), writes=("v2f",))
            p.step("vector", lambda e, ni=ni: e.tensor_scalar(
                f2i[:, :ni, :].rearrange("q i t -> q (i t)"),
                v2f[:, :ni, :].rearrange("q i t -> q (i t)"), -QMAX - 1.0, None,
                ALU.max),
                reads=("v2f", "f2i"), writes=("f2i",))
            p.step("vector", lambda e, par=par, i0=i0, ni=ni: e.tensor_copy(
                xq2[:, par, i0:i0 + ni, :].rearrange("q i t -> q (i t)"),
                f2i[:, :ni, :].rearrange("q i t -> q (i t)")),
                reads=("f2i", f"xq2{par}"), writes=(f"xq2{par}",))

    def p4_mid(hs):
        par = hs % 2
        for hc in range(CC):
            pd = pp[hc % 4][:, :thalf]
            kd = f"pp{hc % 4}"
            p.step("tensor", lambda e, hc=hc, pd=pd, par=par: [
                e.matmul(pd, wdT[:, i2, hc * 128:(hc + 1) * 128],
                         xq2[:, par, i2, :], start=(i2 == 0), stop=(i2 == ic_n - 1))
                for i2 in range(ic_n)][-1],
                reads=(f"xq2{par}", "wdT"), writes=(kd,))
            p.step("scalar", lambda e, hc=hc, pd=pd, par=par: e.activation(
                dn_sb[:, par, hc, :], pd, AF.Identity, scale=swd[:, hc:hc + 1]),
                reads=(kd, "swd2", f"dn{par}_{hc}"), writes=(f"dn{par}_{hc}",))

    def p4_tail(hs):
        par = hs % 2
        dnkeys = tuple(f"dn{par}_{d}" for d in range(CC))
        for g in range(og):
            gg = hs * og + g
            pk = PTK[g % 2]
            p.step("tensor", lambda e, g=g, par=par: [
                e.transpose(pt[g % 2][:, c2 * 128:(c2 + 1) * 128],
                            dn_sb[:, par, c2, g * 128:(g + 1) * 128], ident[:])
                for c2 in range(CC)][-1],
                reads=dnkeys + ("ident",), writes=(pk,))
            p.step("vector", lambda e, g=g, gg=gg: e.tensor_tensor(
                o_pre[:, g, :], h1[:, gg, :], pt[g % 2][:], ALU.add),
                reads=(pk, f"h1_{gg}", f"opre_{g}"), writes=(f"opre_{g}",))
            p.step("vector", lambda e, g=g: [
                e.bn_stats(ostats[:, g, 0, :], o_pre[:, g, 0:512]),
                e.bn_stats(ostats[:, g, 1, :], o_pre[:, g, 512:1024])][-1],
                reads=(f"opre_{g}", f"ost{g}"), writes=(f"ost{g}",))
            p.step("vector", lambda e, g=g: e.bn_aggr(omv[:, g, :], ostats[:, g, :, :]),
                   reads=(f"ost{g}", "omv"), writes=("omv",))
        p.step("scalar", lambda e: e.activation(orsq[:], omv[:, :, 1], AF.Sqrt,
                                                bias=epsc[:]),
               reads=("omv", "epsc", "orsq"), writes=("orsq",))
        p.step("vector", lambda e: e.reciprocal(orsq[:], orsq[:]),
               reads=("orsq",), writes=("orsq",))
        p.step("vector", lambda e: e.tensor_tensor(obeta[:], omv[:, :, 0], orsq[:],
                                                   ALU.mult),
               reads=("orsq", "omv", "obeta"), writes=("obeta",))
        p.step("vector", lambda e: e.tensor_scalar(obeta[:], obeta[:], -1.0, None,
                                                   ALU.mult),
               reads=("obeta",), writes=("obeta",))
        for g in range(og):
            p.step("scalar", lambda e, g=g, par=par: e.activation(
                o_sb[:, par, g, :], o_pre[:, g, :], AF.Identity,
                bias=obeta[:, g:g + 1], scale=orsq[:, g:g + 1]),
                reads=(f"opre_{g}", "orsq", "obeta", f"o{par}"), writes=(f"o{par}",))
        p.dma("sync", f"out{par}",
              [(out_hbm[hs * thalf:(hs + 1) * thalf, :]
                .rearrange("(g q) c -> q g c", q=128), o_sb[:, par])],
              reads=(f"o{par}",), writes=(f"outd{hs}",))

    p4_head(0)
    for hs in range(nch):
        p4_mid(hs)
        if hs + 1 < nch:
            p4_head(hs + 1)
        p4_tail(hs)

    fin = dict(p.dma_tot)
    p.eng_cnt["sync"] += 1
    p.steps.append(("sync", fin, None, "E_sync", False))

    with nc.Block() as block:
        semstack = p.emit_all(nc, block)
    sb_free(*P4_NAMES)
    for name in reversed(list(open_t)):
        open_t.pop(name).__exit__(None, None, None)
    semstack.close()
    return nc


def _wq_slices(wg, wu, wd):
    ic_n = wg.shape[0] // 128
    ngu = 2 * ic_n
    nslot_gu = (ngu + 7) // 8
    nslot_d = (ic_n + 7) // 8
    gus, wds, wdsc = [], [], []
    for r in range(8):
        rows = []
        for j in range(nslot_gu):
            g = r + 8 * j
            if g >= ngu:
                g = r % ngu
            w = wg if g < ic_n else wu
            rc = g if g < ic_n else g - ic_n
            rows.append(w[rc * 128:(rc + 1) * 128, :])
        gus.append(np.ascontiguousarray(np.concatenate(rows, axis=0)))
        colsl = []
        for j in range(nslot_d):
            ic = r + 8 * j
            if ic >= ic_n:
                ic = r % ic_n
            colsl.append(wd[:, ic * 128:(ic + 1) * 128])
        wds.append(np.ascontiguousarray(np.concatenate(colsl, axis=1)))
        wdsc.append(np.ascontiguousarray(wd[r * 128:(r + 1) * 128, :]))
    return gus, wds, wdsc


def _shard_inputs(inputs, tcore):
    x = np.ascontiguousarray(np.asarray(inputs["L0_attn_x"], dtype=np.float32))
    xf = x.reshape(-1, C)
    halves = max(1, (xf.shape[0] // tcore) // max(1, B))
    per_b = xf.shape[0] // B if xf.shape[0] >= B * tcore else tcore
    in_maps = []
    common = {
        "Wk": np.ascontiguousarray(np.asarray(inputs["L0_attn_Wk"], np.float32)),
        "Wv": np.ascontiguousarray(np.asarray(inputs["L0_attn_Wv"], np.float32)),
        "Wr": np.ascontiguousarray(np.asarray(inputs["L0_attn_Wr"], np.float32)),
        "Wo": np.ascontiguousarray(np.asarray(inputs["L0_attn_Wo"], np.float32)),
        "wd": np.ascontiguousarray(np.asarray(inputs["L0_ffn_w_down"], np.float32)),
        "tmk": np.ascontiguousarray(np.asarray(inputs["L0_attn_time_mix_k"], np.float32)),
        "tmv": np.ascontiguousarray(np.asarray(inputs["L0_attn_time_mix_v"], np.float32)),
        "tmr": np.ascontiguousarray(np.asarray(inputs["L0_attn_time_mix_r"], np.float32)),
        "tdecay": np.ascontiguousarray(np.asarray(inputs["L0_attn_time_decay"], np.float32)),
        "tfirst": np.ascontiguousarray(np.asarray(inputs["L0_attn_time_first"], np.float32)),
        "lnw": np.ascontiguousarray(np.asarray(inputs["L0_attn_ln_w"], np.float32)),
        "lnb": np.ascontiguousarray(np.asarray(inputs["L0_attn_ln_b"], np.float32)),
        "clipx": np.full((128, 1), np.float32(np.asarray(inputs["L0_ffn_clip_x"]).ravel()[0]),
                         np.float32),
        "clipd": np.full((128, 1), np.float32(np.asarray(inputs["L0_ffn_clip_down"]).ravel()[0]),
                         np.float32),
    }
    gus, wds, wdsc = _wq_slices(np.asarray(inputs["L0_ffn_w_gate"], np.float32),
                                np.asarray(inputs["L0_ffn_w_up"], np.float32),
                                np.asarray(inputs["L0_ffn_w_down"], np.float32))
    for core in range(NCORES):
        b = core // 2
        h = core % 2
        s = b * T + h * tcore
        xs = np.ascontiguousarray(xf[s:s + tcore])
        if h == 0:
            xb = np.ascontiguousarray(xf[s:s + 1])
            mp = 0.0
        else:
            xb = np.ascontiguousarray(xf[s - 1:s])
            mp = 1.0
        sh = np.zeros((128, 8), np.float32)
        sh[:, core] = 1.0
        cs = np.zeros((128, 8), np.float32)
        if h > 0:
            cs[:, core - 1] = 1.0
        m = dict(common)
        m.update({"x": xs, "xbound": xb,
                  "mprev": np.full((128, 1), mp, np.float32),
                  "selfhot": sh, "csel": cs,
                  "wgu_mine": gus[core], "wd_mine": wds[core],
                  "wdsc_mine": wdsc[core]})
        in_maps.append(m)
    return in_maps


def kernel(**inputs):
    tcore = T * B // NCORES
    in_maps = _shard_inputs(inputs, tcore)
    nc = build_nc(tcore)
    res = run_bass_kernel_spmd(nc, in_maps, core_ids=list(range(NCORES)))
    outs = [np.asarray(r["out"], dtype=np.float32) for r in res.results]
    return np.concatenate(outs, axis=0).reshape(B, T, C)
